# version 25
# baseline (speedup 1.0000x reference)
import contextlib
import numpy as np
import ml_dtypes
import concourse.bass as bass
import concourse.mybir as mybir
from concourse.bass_utils import run_bass_kernel_spmd
from concourse.alu_op_type import AluOpType as OP

F32 = mybir.dt.float32
BF16 = mybir.dt.bfloat16
AF = mybir.ActivationFunctionType
AX = mybir.AxisListType

NCORES = 8
NSEQ = 4
S = 2048
D = 1024
NT = 16
ND = 8
HEADS = 16
HD = 64
NEXP = 9
DI = 1408
NI = 11
FMAX = float(np.finfo(np.float32).max)

ENGS = ("tensor", "scalar", "vector", "sync", "gpsimd")
SEMK = ("pe", "act", "dve", "t0", "t1", "t2", "t3", "w0", "w1", "w2", "w3", "wc", "x", "x0", "x1", "x2", "x3")

BK = 0
BV = 16384
BWA = 33024
BWB = 41216
BOF = 49408
BIG_SZ = 50688
FSLOT = [0, 11264, 22528, 33792]
BH = 45056


class Prog:
    def __init__(self):
        self.ops = {e: [] for e in ENGS}
        self.cnt = {k: 0 for k in SEMK}
        self.waited = {e: {k: 0 for k in SEMK} for e in ENGS}

    def wait(self, eng, key, val):
        if val is None or val <= 0:
            return
        if self.waited[eng][key] >= val:
            return
        self.waited[eng][key] = val
        self.ops[eng].append(("w", key, val))

    def emit(self, eng, fn, waits=(), inc=None, by=1):
        for k, v in waits:
            self.wait(eng, k, v)
        val = None
        if inc is not None:
            self.cnt[inc] += by
            val = self.cnt[inc]
        self.ops[eng].append(("i", fn, inc, by))
        return val

    def marks(self):
        return dict(self.cnt)


def ceil_div(a, b):
    return (a + b - 1) // b


def build_nc(n_seq=NSEQ, only_attn_a=False):
    nc = bass.Bass("TRN2", target_bir_lowering=False)

    dx = nc.dram_tensor("x", [n_seq, S, D], F32, kind="ExternalInput").ap()
    dout = nc.dram_tensor("out", [n_seq, S, D], F32, kind="ExternalOutput").ap()
    wqa = nc.dram_tensor("wqa", [D, D], F32, kind="ExternalInput").ap()
    wka = nc.dram_tensor("wka", [D, D], F32, kind="ExternalInput").ap()
    wva = nc.dram_tensor("wva", [D, D], F32, kind="ExternalInput").ap()
    woa = nc.dram_tensor("woa", [D, D], F32, kind="ExternalInput").ap()
    wqb = nc.dram_tensor("wqb", [D, D], F32, kind="ExternalInput").ap()
    wkb = nc.dram_tensor("wkb", [D, D], F32, kind="ExternalInput").ap()
    wvb = nc.dram_tensor("wvb", [D, D], F32, kind="ExternalInput").ap()
    dbo = nc.dram_tensor("bo", [1, D], F32, kind="ExternalInput").ap()
    ddb = nc.dram_tensor("db", [1, HEADS], F32, kind="ExternalInput").ap()
    dgw = nc.dram_tensor("gwt", [D, 16], F32, kind="ExternalInput").ap()
    dwg = nc.dram_tensor("w_gate", [8, D, DI], F32, kind="ExternalInput").ap()
    dwu = nc.dram_tensor("w_up", [8, D, DI], F32, kind="ExternalInput").ap()
    dwd = nc.dram_tensor("w_down", [8, DI, D], F32, kind="ExternalInput").ap()
    dsg = nc.dram_tensor("sw_gate", [D, DI], F32, kind="ExternalInput").ap()
    dsu = nc.dram_tensor("sw_up", [D, DI], F32, kind="ExternalInput").ap()
    dsd = nc.dram_tensor("sw_down", [DI, D], F32, kind="ExternalInput").ap()
    dmask = nc.dram_tensor("mask", [128, 128], F32, kind="ExternalInput").ap()
    dmaskt = nc.dram_tensor("maskt", [128, 128], F32, kind="ExternalInput").ap()
    didn = nc.dram_tensor("idn", [128, 128], BF16, kind="ExternalInput").ap()

    p = Prog()

    with contextlib.ExitStack() as st:
        sb = lambda name, shape, dt: st.enter_context(nc.sbuf_tensor(name, shape, dt))
        psumt = lambda name, shape: st.enter_context(nc.psum_tensor(name, shape, F32))

        XRES = sb("XRES", [128, NT, D], F32)
        Y = sb("Y", [128, ND, S], BF16)
        BIG = sb("BIG", [128, BIG_SZ], BF16)
        PT_ = sb("PT", [128, 2, 512], BF16)
        OUTTM = sb("OUTTM", [128, D], BF16)
        SIL = sb("SIL", [128, 512], F32)
        LNS = sb("LNS", [128, 4, NT], F32)
        AT = sb("AT", [128, 64], F32)
        NEGM = sb("NEGM", [128, 16], F32)
        U16 = sb("U16", [128, 16], F32)
        T16 = sb("T16", [128, 16], F32)
        SM = sb("SM", [128, 8], F32)
        SM2 = sb("SM2", [128, 8], F32)
        RT = sb("RT", [128, 40], F32)
        CB = sb("CB", [128, NT, 8], F32)
        CBSG = sb("CBSG", [128, NT], F32)
        MASKSB = sb("MASKSB", [128, 128], F32)
        MASKT = sb("MASKT", [128, 128], F32)
        ONESB = sb("ONESB", [128, 1], BF16)
        BOROW = sb("BOROW", [1, D], BF16)
        DBROW = sb("DBROW", [1, HEADS], F32)
        EDBROW = sb("EDBROW", [1, HEADS], F32)
        EDBB = sb("EDBB", [128, HEADS], F32)
        GWT = sb("GWT", [128, ND, 16], BF16)
        ONES = sb("ONES", [1, 128], BF16)
        ONESF = sb("ONESF", [1, 128], F32)
        CONST = sb("CONST", [128, 8], F32)

        PA = psumt("PA", [128, 1024])
        PBp = psumt("PBp", [128, 1024])
        PCD = psumt("PCD", [128, 1024])
        PC = PCD[:, 0:512]
        PD = PCD[:, 512:1024]
        PE2 = psumt("PE2", [128, 512])
        PF = psumt("PF", [128, 512])

        sems = {k: st.enter_context(nc.semaphore("s_" + k)) for k in SEMK}

        QBANKS = [PA[:, 0:512], PA[:, 512:1024], PBp[:, 0:512], PBp[:, 512:1024],
                  PC[:, 0:512], PD[:, 0:512], PE2[:, 0:512], PF[:, 0:512]]
        SBANKS = [PC, PD]

        big = lambda off, ln: BIG[:, off:off + ln]

        def mm(out_ap, lhsT, rhs, start, stop, waits=(), inc=None):
            return p.emit("tensor",
                          lambda h, o=out_ap, l=lhsT, r=rhs, s1=start, s2=stop:
                          h.matmul(o, l, r, start=s1, stop=s2),
                          waits=waits, inc=inc)

        def act(fn, waits=(), inc="act"):
            w = list(waits) + [("act", p.cnt["act"])]
            return p.emit("scalar", fn, waits=w, inc=inc)

        def dve(fn, waits=(), inc="dve"):
            w = list(waits) + [("dve", p.cnt["dve"])]
            return p.emit("vector", fn, waits=w, inc=inc)

        def tp(dst, src, key, waits=()):
            return p.emit("sync",
                          lambda h, d=dst, s=src: h.dma_start_transpose(d, s),
                          waits=waits, inc=key, by=16)

        def wload(dst, src, key, waits=()):
            return p.emit("gpsimd",
                          lambda h, d=dst, s=src: h.dma_start(out=d, in_=s),
                          waits=waits, inc=key, by=16)

        def xdma(dst, src, waits=(), key="x"):
            w = list(waits) + [("act", p.cnt["act"])]
            return p.emit("scalar",
                          lambda h, d=dst, s=src: h.dma_start(out=d, in_=s),
                          waits=w, inc=key, by=16)

        wload(MASKSB[:, 0:64].bitcast(BF16), dmask, "wc")
        wload(MASKT[:, 0:64].bitcast(BF16), dmaskt, "wc")
        wload(BOROW[:], dbo, "wc")
        wload(DBROW[:], ddb, "wc")
        wload(GWT[:], dgw.rearrange("(dc p) e -> p dc e", p=128), "wc")
        w_const = p.cnt["wc"]
        dve(lambda h: h.memset(ONES[:], 1.0))
        dve(lambda h: h.memset(ONESF[:], 1.0))
        dve(lambda h: h.memset(CONST[:, 0:1], 1e-5))
        dve(lambda h: h.memset(CONST[:, 1:2], -0.5))
        dve(lambda h: h.memset(CONST[:, 2:3], 0.125))
        dve(lambda h: h.memset(CONST[:, 3:4], 1.0 / 32.0))
        dve(lambda h: h.memset(CONST[:, 4:5], -30.0))
        dve(lambda h: h.memset(CONST[:, 5:6], -1.0))
        dve(lambda h: h.memset(ONESB[:], 1.0))
        ones_ready = p.cnt["dve"]
        edb_v = act(lambda h: h.activation(out=EDBROW[:], in_=DBROW[:], func=AF.Exp),
                    waits=[("wc", w_const)])
        mm(PC[:, 0:16], ONESF[:], EDBROW[:], True, True,
           waits=[("dve", ones_ready), ("act", edb_v)], inc="pe")
        pe_edb = p.cnt["pe"]
        act(lambda h: h.copy(out=EDBB[:], in_=PC[:, 0:16]), waits=[("pe", pe_edb)])
        edbb_ready = p.cnt["act"]

        state = {
            "xres_dve": 0,
            "x_load": 0,
            "pc_free": ("act", edbb_ready),
        }

        free = {
            "PA0": ("pe", 0), "PA1": ("pe", 0), "PB0": ("pe", 0), "PB1": ("pe", 0),
            "PC": ("act", edbb_ready), "PD": ("pe", 0), "PE2": ("pe", 0), "PF": ("pe", 0),
            "PTS0": ("pe", 0), "PTS1": ("pe", 0), "PTS2": ("pe", 0), "PTS3": ("pe", 0),
            "OUTTM": ("pe", 0), "SIL0": ("pe", 0), "SIL1": ("pe", 0),
            "Y": ("pe", 0), "BIG": ("pe", 0), "XRES_w": ("x", 0),
            "WA": ("pe", 0), "WB": ("pe", 0), "OF": ("pe", 0),
            "SLOT0": ("pe", 0), "SLOT1": ("pe", 0), "SLOT2": ("pe", 0), "SLOT3": ("pe", 0),
            "H": ("pe", 0),
        }

        def emit_ln(xres_rdy, y_free_pe, m_tp):
            yflat = Y[:].rearrange("p a b -> p (a b)")
            sq_scr = yflat[:, 15 * 1024:16 * 1024]
            ysc = [OUTTM[:], PT_[:].rearrange("p a b -> p (a b)")]
            for tc in range(NT):
                act(lambda h, tc=tc, o=sq_scr: h.activation(
                    out=o, in_=XRES[:, tc, :], func=AF.Square,
                    accum_out=LNS[:, 1, tc:tc + 1]),
                    waits=[xres_rdy[tc], ("pe", y_free_pe)])
                dve(lambda h, tc=tc, o=yflat[:, 14 * 1024:15 * 1024]: h.tensor_scalar(
                    out=o, in0=XRES[:, tc, :], scalar1=0.0, scalar2=0.0,
                    op0=OP.add, op1=OP.add, accum_out=LNS[:, 0, tc:tc + 1]),
                    waits=[xres_rdy[tc], ("pe", y_free_pe)])
            sumsq_done = p.cnt["act"]
            dve(lambda h: h.tensor_scalar(out=LNS[:, 2, :], in0=LNS[:, 0, :],
                                          scalar1=-1.0 / D, scalar2=None, op0=OP.mult))
            dve(lambda h: h.tensor_tensor(out=LNS[:, 0, :], in0=LNS[:, 2, :],
                                          in1=LNS[:, 2, :], op=OP.mult))
            var_v = dve(lambda h: h.scalar_tensor_tensor(
                out=LNS[:, 1, :], in0=LNS[:, 1, :], scalar=1.0 / D,
                in1=LNS[:, 0, :], op0=OP.mult, op1=OP.subtract),
                waits=[("act", sumsq_done)])
            lnv = act(lambda h: h.activation(out=LNS[:, 0, :], in_=LNS[:, 1, :],
                                             func=AF.Ln, bias=CONST[:, 0:1]),
                      waits=[("dve", var_v), ("dve", ones_ready)])
            rstd_v = act(lambda h: h.activation(out=LNS[:, 3, :], in_=LNS[:, 0, :],
                                                func=AF.Exp, scale=CONST[:, 1:2]))
            y_tp = []
            tp_of_tc = []
            IDN = big(50432, 128)
            idn_ld = wload(IDN, didn, "wc",
                           waits=[("pe", y_free_pe)] + list(m_tp))
            tp_bank = lambda i: (PA if i < 16 else PBp)[:, (i % 16) * 64:(i % 16) * 64 + 64].bitcast(BF16)
            evac_hist = []
            for tc in range(NT):
                scr = ysc[tc % 2]
                w_tp = [tp_of_tc[tc - 2]] if tc >= 2 else m_tp
                nv = dve(lambda h, tc=tc, o=scr: h.tensor_scalar(
                    out=o, in0=XRES[:, tc, :],
                    scalar1=LNS[:, 2, tc:tc + 1], scalar2=LNS[:, 3, tc:tc + 1],
                    op0=OP.add, op1=OP.mult),
                    waits=[("act", rstd_v), ("pe", y_free_pe)] + list(w_tp))
                last_tp = None
                for dc in range(ND):
                    ti = (tc % 2) * 8 + dc
                    wv = [("dve", nv), ("wc", idn_ld)]
                    if tc < 2:
                        wv += list(m_tp)
                    if len(evac_hist) >= 2:
                        wv.append(("act", evac_hist[-2]))
                    last_tp = p.emit("tensor",
                                     lambda h, o=tp_bank(ti),
                                     i_=scr[:, dc * 128:(dc + 1) * 128], idn=IDN:
                                     h.transpose(o, i_, idn),
                                     waits=wv, inc="pe")
                ev = None
                for dc in range(ND):
                    ti = (tc % 2) * 8 + dc
                    ev = act(lambda h, tc=tc, dc=dc, i_=tp_bank(ti): h.copy(
                        out=Y[:, dc, tc * 128:(tc + 1) * 128], in_=i_),
                        waits=[("pe", last_tp)])
                evac_hist.append(ev)
                tp_of_tc.append(("act", ev))
            for tt in range(4):
                y_tp.append([tp_of_tc[tt * 4 + i] for i in range(4)])
            for nm in ("PA0", "PA1", "PB0", "PB1"):
                free[nm] = ("act", p.cnt["act"])
            ln_end = [("act", p.cnt["act"])]
            return y_tp, tp_of_tc, ln_end

        def emit_proj(wk_d, wv_d, wq_d, y_tp, qscale_ap):
            wa_ld = wload(big(BWA, 8192).rearrange("p (dc o) -> p dc o", o=D),
                          wk_d.rearrange("(dc p) o -> p dc o", p=128), "w0",
                          waits=[free["WA"]])
            wb_ld = wload(big(BWB, 8192).rearrange("p (dc o) -> p dc o", o=D),
                          wv_d.rearrange("(dc p) o -> p dc o", p=128), "w1",
                          waits=[free["WB"]])
            dve(lambda h: h.memset(
                bass.AP(BIG, BV + 64, [[BIG_SZ, 128], [1040, 16], [65, 16]]), 1.0),
                waits=[free["BV_ones"]] if "BV_ones" in free else [])
            ones_v = p.cnt["dve"]
            kbank_free = [("pe", 0)] * 4
            kv = 0
            for tt in range(4):
                for oc in range(ND):
                    bidx = (tt * ND + oc) % 4
                    bank = [PC, PD, PE2, PF][bidx]
                    waits = [("w0", wa_ld), kbank_free[bidx],
                             free[["PC", "PD", "PE2", "PF"][bidx]]] + list(y_tp[tt])
                    for dc in range(ND):
                        g = mm(bank[:, 0:512],
                               big(BWA, 8192)[:, dc * D + oc * 128: dc * D + (oc + 1) * 128],
                               Y[:, dc, tt * 512:(tt + 1) * 512],
                               dc == 0, dc == ND - 1,
                               waits=waits if dc == 0 else (),
                               inc="pe" if dc == ND - 1 else None)
                    kv = act(lambda h, oc=oc, tt=tt, b=bank: h.copy(
                        out=big(BK, 16384).rearrange("p (c t) -> p c t", c=ND)[:, oc, tt * 512:(tt + 1) * 512],
                        in_=b[:, 0:512]), waits=[("pe", g)])
                    kbank_free[bidx] = ("act", kv)
            k_done = kv
            kproj_pe = p.cnt["pe"]
            vfree = [free["PA0"], free["PA1"], free["PB0"], free["PB1"]]
            vv = 0
            for tc in range(NT):
                bank = PA if tc % 2 == 0 else PBp
                bw = [vfree[0], vfree[1]] if tc % 2 == 0 else [vfree[2], vfree[3]]
                gl = None
                for half in range(2):
                    for dc in range(ND):
                        gl = mm(bank[:, half * 512:(half + 1) * 512],
                                Y[:, dc, tc * 128:(tc + 1) * 128],
                                big(BWB, 8192)[:, dc * D + half * 512: dc * D + (half + 1) * 512],
                                dc == 0, dc == ND - 1,
                                waits=([("w1", wb_ld), bw[half]] if dc == 0 else ()),
                                inc="pe" if dc == ND - 1 else None)
                vv = act(lambda h, tc=tc, b=bank: h.copy(
                    out=bass.AP(BIG, BV + tc * 1040, [[BIG_SZ, 128], [65, 16], [1, 64]]),
                    in_=b[:]), waits=[("pe", gl)])
                if tc % 2 == 0:
                    vfree[0] = vfree[1] = ("act", vv)
                else:
                    vfree[2] = vfree[3] = ("act", vv)
            v_done = vv
            vproj_pe = p.cnt["pe"]
            wq_ld = wload(big(BWA, 8192).rearrange("p (dc o) -> p dc o", o=D),
                          wq_d.rearrange("(dc p) o -> p dc o", p=128), "w0",
                          waits=[("pe", kproj_pe)])
            evac_of_bank = [("act", 0)] * 8
            for tt in range(4):
                last_g = None
                for oc in range(ND):
                    bank = QBANKS[oc]
                    waits = [("w0", wq_ld), evac_of_bank[oc]]
                    if tt == 0:
                        waits.append(("act", v_done))
                        waits.append(("act", k_done))
                    for dc in range(ND):
                        last_g = mm(bank, big(BWA, 8192)[:, dc * D + oc * 128: dc * D + (oc + 1) * 128],
                                    Y[:, dc, tt * 512:(tt + 1) * 512],
                                    dc == 0, dc == ND - 1,
                                    waits=waits if dc == 0 else (),
                                    inc="pe" if (dc == ND - 1 and oc == ND - 1) else None)
                for oc in range(ND):
                    ev = act(lambda h, oc=oc, tt=tt, b=QBANKS[oc]: h.activation(
                        out=Y[:, oc, tt * 512:(tt + 1) * 512], in_=b,
                        func=AF.Copy, scale=qscale_ap),
                        waits=[("pe", last_g), ("dve", ones_ready)])
                    evac_of_bank[oc] = ("act", ev)
            q_done = p.cnt["act"]
            for oc, nm in enumerate(["PA0", "PA1", "PB0", "PB1",
                                     "PC", "PD", "PE2", "PF"]):
                free[nm] = evac_of_bank[oc]
            return dict(k_done=k_done, v_done=v_done, q_done=q_done,
                        vproj_pe=vproj_pe, kproj_pe=kproj_pe,
                        qproj_pe=p.cnt["pe"], ones_v=ones_v)

        PTF = PT_[:].rearrange("p a b -> p (a b)")

        def pt_slot(c, par, lo, ln):
            if par == 0:
                return PTF[:, c * 512 + lo: c * 512 + lo + ln]
            return SIL[:, (c * 512 + lo) // 2: (c * 512 + lo + ln) // 2].bitcast(BF16)

        def emit_attn(kind, ms, wo_d, ln_end_tp, m_dve=0):
            kview = big(BK, 16384).rearrange("p (c t) -> p c t", c=ND)
            v65 = lambda kc, h_i: BIG[:, BV + kc * 1040 + h_i * 65: BV + kc * 1040 + h_i * 65 + 65]
            vb_half = lambda kc, half: bass.AP(
                BIG, BV + kc * 1040 + half * 8 * 65,
                [[BIG_SZ, 128], [65, 8], [1, 64]])
            vb_one = lambda kc: BIG[:, BV + kc * 1040 + 64: BV + kc * 1040 + 65]
            qk_w = [("act", ms["q_done"]), ("act", ms["k_done"])]
            resid = []
            if kind == "a":
                wo_ld = wload(big(BWB, 8192).rearrange("p (dc o) -> p dc o", o=D),
                              wo_d.rearrange("(dc p) o -> p dc o", p=128), "w1",
                              waits=[("pe", ms["vproj_pe"])])
                sfree = {(0, 0): [free["PC"]], (0, 1): [free["PA0"]],
                         (1, 0): [free["PD"]], (1, 1): [free["PA1"]]}
                hfree = [free["PB0"], free["PB1"]]
                ptfree = {(0, 0): [free["PTS0"], free["PTS1"]] + list(ln_end_tp),
                          (1, 0): [free["PTS2"], free["PTS3"]] + list(ln_end_tp),
                          (0, 1): [("dve", m_dve)],
                          (1, 1): [("dve", m_dve)]}
                accfree = [[free["PE2"]], [free["PF"]]]
                pafree = [[free["PA0"]], [free["PA1"]]]
                outtm_free = list(ln_end_tp)
                slot_i = 0
                pend_op = None

                def emit_oproj(qc_, of_tp_):
                    zg = None
                    for half in range(2):
                        for dc in range(ND):
                            mm(PA[:, half * 512:(half + 1) * 512],
                               big(BOF, 1024)[:, dc * 128:(dc + 1) * 128],
                               big(BWB, 8192)[:, dc * D + half * 512: dc * D + (half + 1) * 512],
                               dc == 0, False,
                               waits=(([("t2", of_tp_), ("w1", wo_ld),
                                        ("dve", ones_ready), ("wc", w_const)]
                                       + pafree[half]
                                       + sfree[(half, 1)])
                                      if dc == 0 else ()))
                        zg = mm(PA[:, half * 512:(half + 1) * 512], ONES[:],
                                BOROW[:, half * 512:(half + 1) * 512],
                                False, True, inc="pe")
                    free["OF"] = ("pe", p.cnt["pe"])
                    rv = dve(lambda h, qc_=qc_: h.tensor_tensor(
                        out=XRES[:, qc_, :], in0=XRES[:, qc_, :], in1=PA[:],
                        op=OP.add), waits=[("pe", zg)])
                    pafree[0] = pafree[1] = [("dve", rv)]
                    sfree[(0, 1)] = [("dve", rv)]
                    sfree[(1, 1)] = [("dve", rv)]
                    resid.append(("dve", rv))

                for qc in range(NT):
                    klen = (qc + 1) * 128
                    nkt = ceil_div(klen, 512)
                    widths = [min(512, klen - 512 * j) for j in range(nkt)]
                    nck = qc + 1
                    for hgp in range(2):
                        hgs = (2 * hgp, 2 * hgp + 1)
                        ph1_banks = [PBp[:, 0:512], PBp[:, 512:1024], PC, PD]
                        ph1_pairs = [PBp[:], PCD[:]]
                        ph1_free = [hfree[0], hfree[1],
                                    sfree[(0, 0)][0], sfree[(1, 0)][0]]
                        bctr = 0
                        for hg in hgs:
                            for j4 in range(4):
                                h_i = hg * 4 + j4
                                pbase, oc_h = 64 * (h_i % 2), h_i // 2
                                q_ap = Y[pbase:pbase + 64, oc_h, qc * 128:(qc + 1) * 128]
                                if nkt <= 2:
                                    pr = h_i % 2
                                    pair = ph1_pairs[pr]
                                    g = None
                                    for kt in range(nkt):
                                        w_ = widths[kt]
                                        g = mm(pair[:, kt * 512:kt * 512 + w_],
                                               q_ap,
                                               kview[pbase:pbase + 64, oc_h, kt * 512:kt * 512 + w_],
                                               True, kt != nkt - 1,
                                               waits=qk_w + [ph1_free[pr * 2],
                                                             ph1_free[pr * 2 + 1]]
                                                     if kt == 0 else qk_w,
                                               inc="pe" if kt == nkt - 1 else None)
                                    gm = mm(pair[:, klen - 128:klen],
                                            big(50432, 128),
                                            MASKSB[:, 0:64].bitcast(BF16),
                                            False, True,
                                            waits=[("pe", g), ("wc", w_const)],
                                            inc="pe")
                                    mx = dve(lambda h, h_i=h_i, b=pair, w_=klen:
                                             h.tensor_reduce(
                                                 out=NEGM[:, h_i:h_i + 1],
                                                 in_=b[:, 0:w_], axis=AX.X,
                                                 op=OP.max, negate=True),
                                             waits=[("pe", gm)])
                                    ph1_free[pr * 2] = ("dve", mx)
                                    ph1_free[pr * 2 + 1] = ("dve", mx)
                                    continue
                                for kt in range(nkt):
                                    w_ = widths[kt]
                                    bsel = bctr % 4
                                    bctr += 1
                                    bank = ph1_banks[bsel]
                                    g = mm(bank[:, 0:w_], q_ap,
                                           kview[pbase:pbase + 64, oc_h, kt * 512:kt * 512 + w_],
                                           True, kt != nkt - 1,
                                           waits=qk_w + [ph1_free[bsel]], inc="pe")
                                    wl = [("pe", g)]
                                    if kt == nkt - 1:
                                        gm = mm(bank[:, w_ - 128:w_],
                                                big(50432, 128),
                                                MASKSB[:, 0:64].bitcast(BF16),
                                                False, True,
                                                waits=[("pe", g), ("wc", w_const)],
                                                inc="pe")
                                        wl = [("pe", gm)]
                                    mx = dve(lambda h, kt=kt, h_i=h_i, b=bank, w_=w_:
                                             h.tensor_reduce(
                                                 out=AT[:, h_i * 4 + kt:h_i * 4 + kt + 1],
                                                 in_=b[:, 0:w_], axis=AX.X, op=OP.max),
                                             waits=wl)
                                    ph1_free[bsel] = ("dve", mx)
                                dve(lambda h, h_i=h_i, n=nkt: h.tensor_reduce(
                                    out=NEGM[:, h_i:h_i + 1],
                                    in_=AT[:, h_i * 4:h_i * 4 + n],
                                    axis=AX.X, op=OP.max, negate=True))
                        hfree[0], hfree[1] = ph1_free[0], ph1_free[1]
                        sfree[(0, 0)] = [ph1_free[2]]
                        sfree[(1, 0)] = [ph1_free[3]]
                        negm8 = p.cnt["dve"]
                        uu = act(lambda h, hg=hgs[0]: h.activation(
                            out=U16[:, hg * 4:hg * 4 + 8],
                            in_=NEGM[:, hg * 4:hg * 4 + 8], func=AF.Exp,
                            scale=CONST[:, 5:6], bias=CONST[:, 4:5]),
                            waits=[("dve", negm8), ("dve", ones_ready)])
                        t16 = dve(lambda h, hg=hgs[0]: h.tensor_tensor(
                            out=T16[:, hg * 4:hg * 4 + 8],
                            in0=U16[:, hg * 4:hg * 4 + 8],
                            in1=EDBB[:, hg * 4:hg * 4 + 8], op=OP.mult),
                            waits=[("act", uu), ("act", edbb_ready)])
                        if hgp == 0 and pend_op is not None:
                            emit_oproj(*pend_op)
                            pend_op = None
                        jpos = {0: 0, 2: 128, 1: 256, 3: 384}
                        abank_of = {0: PE2, 1: PF}
                        pv_last = {0: None, 1: None}
                        pend_pv = None

                        def emit_pvs(job):
                            kcp, parp, exd = job
                            for c in (0, 1):
                                wpv = [("act", exd[c])]
                                if kcp == 0:
                                    wpv += [("act", ms["v_done"]),
                                            ("dve", ms["ones_v"])] + accfree[c]
                                for j in range(4):
                                    h_i = hgs[c] * 4 + j
                                    lhs = pt_slot(c, parp, jpos[j], 128)
                                    pv_last[c] = mm(
                                        abank_of[c][:, j * 128:j * 128 + 65], lhs,
                                        v65(kcp, h_i),
                                        kcp == 0 and j == 0,
                                        kcp == nck - 1 and j == 3,
                                        waits=wpv if j == 0 else (),
                                        inc="pe" if (kcp == nck - 1 and j == 3) else None)
                                ptfree[(c, parp)] = [("pe", pv_last[c])]

                        for kc in range(nck):
                            par = kc % 2
                            bank_of = {0: PC if par == 0 else PA[:, 0:512],
                                       1: PD if par == 0 else PA[:, 512:1024]}
                            mm_done = {}
                            for c in (0, 1):
                                bank = bank_of[c]
                                g0 = None
                                for j in (0, 2):
                                    h_i = hgs[c] * 4 + j
                                    oc_h = h_i // 2
                                    g0 = mm(bank[:, jpos[j]:jpos[j] + 128],
                                            kview[0:64, oc_h, kc * 128:(kc + 1) * 128],
                                            Y[0:64, oc_h, qc * 128:(qc + 1) * 128],
                                            j == 0, False,
                                            waits=list(sfree[(c, par)]) if j == 0 else (),
                                            inc="pe" if j == 2 else None)
                                g1 = None
                                for j in (1, 3):
                                    h_i = hgs[c] * 4 + j
                                    oc_h = h_i // 2
                                    g1 = mm(bank[:, jpos[j]:jpos[j] + 128],
                                            kview[64:128, oc_h, kc * 128:(kc + 1) * 128],
                                            Y[64:128, oc_h, qc * 128:(qc + 1) * 128],
                                            False, j == 3 and kc != qc,
                                            waits=[("pe", g0)] if j == 1 else (),
                                            inc="pe" if j == 3 else None)
                                mm_done[c] = ("pe", g1)
                            if kc == qc:
                                for c in (0, 1):
                                    gd = None
                                    for j in range(4):
                                        gd = mm(bank_of[c][:, jpos[j]:jpos[j] + 128],
                                                big(50432, 128),
                                                MASKT[:, 0:64].bitcast(BF16),
                                                False, j == 3,
                                                waits=[mm_done[c], ("wc", w_const)]
                                                      if j == 0 else (),
                                                inc="pe" if j == 3 else None)
                                    mm_done[c] = ("pe", gd)
                            src = PCD[:] if par == 0 else PA[:]
                            dst = (PTF[:, 0:1024] if par == 0
                                   else SIL[:, 0:512].bitcast(BF16))
                            ex = act(lambda h, b=src, o=dst: h.activation(
                                out=o, in_=b, func=AF.Exp, bias=CONST[:, 4:5]),
                                waits=[mm_done[0], mm_done[1], ("dve", ones_ready)]
                                      + list(ptfree[(0, par)]) + list(ptfree[(1, par)]))
                            sfree[(0, par)] = [("act", ex)]
                            sfree[(1, par)] = [("act", ex)]
                            exd = {0: ex, 1: ex}
                            if pend_pv is not None:
                                emit_pvs(pend_pv)
                            pend_pv = (kc, par, exd)
                        emit_pvs(pend_pv)
                        ot = None
                        for c in (0, 1):
                            for j in range(4):
                                h_i = hgs[c] * 4 + j
                                dve(lambda h, h_i=h_i, j=j, c=c, a=abank_of[c]:
                                    h.tensor_tensor(
                                        out=SM[:, c * 4 + j:c * 4 + j + 1],
                                        in0=T16[:, h_i:h_i + 1],
                                        in1=a[:, j * 128 + 64:j * 128 + 65],
                                        op=OP.add),
                                    waits=[("dve", t16), ("pe", pv_last[c])])
                        rec = dve(lambda h: h.reciprocal(out=SM2[:], in_=SM[:]))
                        for c in (0, 1):
                            for j in range(4):
                                h_i = hgs[c] * 4 + j
                                ot = act(lambda h, h_i=h_i, j=j, c=c, a=abank_of[c]:
                                         h.activation(
                                             out=OUTTM[:, 64 * h_i:64 * h_i + 64],
                                             in_=a[:, j * 128:j * 128 + 64],
                                             func=AF.Copy,
                                             scale=SM2[:, c * 4 + j:c * 4 + j + 1]),
                                         waits=[("dve", rec)] + list(outtm_free))
                            accfree[c] = [("act", ot)]
                    heads_done = p.cnt["act"]
                    for dc in range(ND):
                        tp(big(BOF, 1024)[:, dc * 128:(dc + 1) * 128],
                           OUTTM[:, dc * 128:(dc + 1) * 128], "t2",
                           waits=[("act", heads_done), free["OF"]])
                    of_tp = p.cnt["t2"]
                    outtm_free = [("t2", of_tp)]
                    pend_op = (qc, of_tp)
                if pend_op is not None:
                    emit_oproj(*pend_op)
                free["PC"] = sfree[(0, 0)][0]
                free["PD"] = sfree[(1, 0)][0]
                free["PB0"] = hfree[0]
                free["PB1"] = hfree[1]
                free["PE2"] = accfree[0][0]
                free["PF"] = accfree[1][0]
                free["PA0"] = pafree[0][0]
                free["PA1"] = pafree[1][0]
                free["PTS0"] = free["PTS1"] = ptfree[(0, 0)][0]
                free["PTS2"] = free["PTS3"] = ptfree[(1, 0)][0]
            else:
                SB4 = [PC[:, 0:128], PD[:, 0:128],
                       PBp[:, 0:128], PBp[:, 512:640]]
                sfree = [free["PC"], free["PD"], free["PB0"], free["PB1"]]
                ptfree = [[free["PTS%d" % (i % 4)]] + list(ln_end_tp)
                          if i < 4 else [("dve", m_dve)] for i in range(8)]
                pffree = [free["PF"]]
                pafree = [free["PA0"], free["PA1"]]
                for qc in range(NT):
                    nck = qc + 1
                    pv_g = None
                    for kc in range(nck):
                        bank = SB4[kc % 4]
                        g = None
                        for dc in range(ND):
                            g = mm(bank,
                                   kview[:, dc, kc * 128:(kc + 1) * 128],
                                   Y[:, dc, qc * 128:(qc + 1) * 128],
                                   dc == 0, dc == ND - 1 and kc != qc,
                                   waits=([sfree[kc % 4]] + qk_w if dc == 0 else ()),
                                   inc="pe" if (dc == ND - 1 and kc != qc) else None)
                        lastw = ("pe", g)
                        if kc == qc:
                            g = mm(bank, big(50432, 128),
                                   MASKT[:, 0:64].bitcast(BF16),
                                   False, True, waits=[("wc", w_const)],
                                   inc="pe")
                            lastw = ("pe", g)
                        slot = kc % 8
                        po = pt_slot(slot // 4, slot % 4 // 2, (slot % 2) * 256, 128) \
                            if False else (
                            PTF[:, (slot % 4) * 128:(slot % 4) * 128 + 128] if slot < 4
                            else SIL[:, (slot - 4) * 64:(slot - 4) * 64 + 64].bitcast(BF16))
                        ex = act(lambda h, b=bank, o=po: h.activation(
                            out=o, in_=b,
                            func=AF.Exp, bias=CONST[:, 4:5]),
                            waits=[lastw, ("dve", ones_ready)] + list(ptfree[slot]))
                        sfree[kc % 4] = ("act", ex)
                        wpv = [("act", ex)]
                        if kc == 0:
                            wpv += [pafree[0], pafree[1],
                                    ("act", ms["v_done"])] + list(pffree)
                        lhs = po
                        for half in range(2):
                            mm(PA[:, half * 512:(half + 1) * 512], lhs,
                               vb_half(kc, half),
                               kc == 0, kc == nck - 1,
                               waits=wpv if half == 0 else ())
                        pv_g = mm(PF[:, 0:1], lhs, ONESB[:],
                                  kc == 0, kc == nck - 1, inc="pe")
                        ptfree[slot] = [("pe", pv_g)]
                    rec = dve(lambda h: h.reciprocal(out=SM[:, 3:4], in_=PF[:, 0:1]),
                              waits=[("pe", pv_g)])
                    pffree = [("dve", rec)]
                    rv = dve(lambda h, qc=qc: h.scalar_tensor_tensor(
                        out=XRES[:, qc, :], in0=PA[:],
                        scalar=SM[:, 3:4], in1=XRES[:, qc, :],
                        op0=OP.mult, op1=OP.add))
                    pafree = [("dve", rv), ("dve", rv)]
                    resid.append(("dve", rv))
                free["PC"] = sfree[0]
                free["PD"] = sfree[1]
                free["PB0"] = sfree[2]
                free["PB1"] = sfree[3]
                free["PA0"] = pafree[0]
                free["PA1"] = pafree[1]
                free["PF"] = pffree[0]
                for i in range(4):
                    free["PTS%d" % i] = ptfree[i][0]
                free["SILB"] = ptfree[4][0]
            return resid

        def emit_ffn(y_tp, tp_tc, slot_wait_pe):
            kexp = lambda e, j: (3 * e + j) % 4
            slot_ld = {}
            srcs = lambda e: [(dwg[e] if e < 8 else dsg),
                              (dwu[e] if e < 8 else dsu),
                              (dwd[e] if e < 8 else dsd)]
            for j in range(3):
                s_i = kexp(0, j)
                rear = "(hc p) i -> p hc i" if j < 2 else "(ic p) o -> p ic o"
                slot_ld[s_i] = wload(
                    big(FSLOT[s_i], 11264).rearrange("p (a b) -> p a b", a=(ND if j < 2 else NI)),
                    srcs(0)[j].rearrange(rear, p=128), "w%d" % s_i,
                    waits=[("pe", slot_wait_pe)])
            slot_free = {i: ("pe", slot_wait_pe) for i in range(4)}

            cb_dve = [0] * NT
            sg_act = [0] * NT

            def emit_router(tc, rb, bfree):
                g = None
                for dc in range(ND):
                    g = mm(rb[:, 0:16], Y[:, dc, tc * 128:(tc + 1) * 128],
                           GWT[:, dc, :], dc == 0, dc == ND - 1,
                           waits=([("wc", w_const), tp_tc[tc], bfree]
                                  if dc == 0 else ()),
                           inc="pe" if dc == ND - 1 else None)
                nm = dve(lambda h, rb=rb: h.tensor_reduce(
                    out=RT[:, 32:33], in_=rb[:, 0:8], axis=AX.X, op=OP.max,
                    negate=True), waits=[("pe", g)])
                ex = act(lambda h, rb=rb: h.activation(
                    out=RT[:, 0:8], in_=rb[:, 0:8],
                    func=AF.Exp, bias=RT[:, 32:33]),
                    waits=[("dve", nm)])
                sg = act(lambda h, tc=tc, rb=rb: h.activation(
                    out=CBSG[:, tc:tc + 1], in_=rb[:, 8:9], func=AF.Sigmoid))
                sg_act[tc] = sg
                dve(lambda h: h.tensor_reduce(out=RT[:, 33:34], in_=RT[:, 0:8],
                                              axis=AX.X, op=OP.max),
                    waits=[("act", ex)])
                dve(lambda h: h.tensor_scalar(out=RT[:, 8:16], in0=RT[:, 0:8],
                                              scalar1=RT[:, 33:34], scalar2=None,
                                              op0=OP.is_ge))
                dve(lambda h: h.scalar_tensor_tensor(
                    out=RT[:, 16:24], in0=RT[:, 8:16], scalar=-1e30,
                    in1=RT[:, 0:8], op0=OP.mult, op1=OP.add))
                dve(lambda h: h.tensor_reduce(out=RT[:, 34:35], in_=RT[:, 16:24],
                                              axis=AX.X, op=OP.max))
                dve(lambda h: h.tensor_scalar(out=RT[:, 24:32], in0=RT[:, 16:24],
                                              scalar1=RT[:, 34:35], scalar2=None,
                                              op0=OP.is_ge))
                dve(lambda h: h.tensor_tensor(out=RT[:, 8:16], in0=RT[:, 8:16],
                                              in1=RT[:, 24:32], op=OP.add))
                dve(lambda h: h.tensor_tensor(out=RT[:, 33:34], in0=RT[:, 33:34],
                                              in1=RT[:, 34:35], op=OP.add))
                dve(lambda h: h.reciprocal(out=RT[:, 35:36], in_=RT[:, 33:34]))
                cb_dve[tc] = dve(lambda h, tc=tc: h.scalar_tensor_tensor(
                    out=CB[:, tc, :], in0=RT[:, 0:8], scalar=RT[:, 35:36],
                    in1=RT[:, 8:16], op0=OP.mult, op1=OP.mult))
                return ("act", sg)

            gb = [PC, PE2]
            ub = [PD, PF]
            gfree = [free["PC"], free["PE2"]]
            ufree = [free["PD"], free["PF"]]
            dfree = [free["PA0"], free["PB0"]]
            silfree = [free["SIL0"]]
            hfree = ("pe", slot_wait_pe)
            resid = [None] * NT
            for e in range(NEXP):
                wg_s, wu_s, wd_s = (kexp(e, 0), kexp(e, 1), kexp(e, 2))
                wg_last = wu_last = wd_last = None
                for tt in range(4):
                    if e == 0:
                        for tcl in range(4):
                            tc_r = tt * 4 + tcl
                            fr = emit_router(tc_r, gb[0] if tcl % 2 == 0 else ub[0],
                                             gfree[0] if tcl % 2 == 0 else ufree[0])
                            if tcl % 2 == 0:
                                gfree[0] = fr
                            else:
                                ufree[0] = fr
                    h_dve = []
                    for ic in range(NI):
                        par = ic % 2
                        g = None
                        for hc in range(ND):
                            g = mm(gb[par][:, 0:512],
                                   big(FSLOT[wg_s], 11264)[:, hc * DI + ic * 128: hc * DI + (ic + 1) * 128],
                                   Y[:, hc, tt * 512:(tt + 1) * 512],
                                   hc == 0, hc == ND - 1,
                                   waits=(([("w%d" % wg_s, slot_ld[wg_s]),
                                           gfree[par], hfree] + list(y_tp[tt])) if hc == 0 else ()),
                                   inc="pe" if hc == ND - 1 else None)
                        wg_last = g
                        u = None
                        for hc in range(ND):
                            u = mm(ub[par][:, 0:512],
                                   big(FSLOT[wu_s], 11264)[:, hc * DI + ic * 128: hc * DI + (ic + 1) * 128],
                                   Y[:, hc, tt * 512:(tt + 1) * 512],
                                   hc == 0, hc == ND - 1,
                                   waits=([("w%d" % wu_s, slot_ld[wu_s]), ufree[par]]
                                          if hc == 0 else ()),
                                   inc="pe" if hc == ND - 1 else None)
                        wu_last = u
                        sl = act(lambda h, par=par: h.activation(
                            out=SIL[:], in_=gb[par][:, 0:512],
                            func=AF.Silu), waits=[("pe", g), silfree[0]])
                        gfree[par] = ("act", sl)
                        hv = dve(lambda h, ic=ic, par=par: h.tensor_tensor(
                            out=big(BH, NI * 512)[:, ic * 512:(ic + 1) * 512],
                            in0=SIL[:], in1=ub[par][:, 0:512],
                            op=OP.mult),
                            waits=[("act", sl), ("pe", u), hfree])
                        ufree[par] = ("dve", hv)
                        silfree[0] = ("dve", hv)
                        h_dve.append(hv)
                    h_ready = h_dve[-1]
                    for tcl in range(4):
                        tc_g = tt * 4 + tcl
                        par = tc_g % 2
                        bank = PA if par == 0 else PBp
                        g2 = None
                        for half in range(2):
                            for ic in range(NI):
                                g2 = mm(bank[:, half * 512:(half + 1) * 512],
                                        big(BH, NI * 512)[:, ic * 512 + tcl * 128: ic * 512 + (tcl + 1) * 128],
                                        big(FSLOT[wd_s], 11264)[:, ic * D + half * 512: ic * D + (half + 1) * 512],
                                        ic == 0, ic == NI - 1,
                                        waits=([("dve", h_ready),
                                                ("w%d" % wd_s, slot_ld[wd_s]),
                                                dfree[par]] if ic == 0 else ()),
                                        inc="pe" if ic == NI - 1 else None)
                        wd_last = g2
                        scal = (CB[:, tc_g, e:e + 1] if e < 8
                                else CBSG[:, tc_g:tc_g + 1])
                        rv = dve(lambda h, tc_g=tc_g, b=bank, s=scal: h.scalar_tensor_tensor(
                            out=XRES[:, tc_g, :], in0=b[:], scalar=s,
                            in1=XRES[:, tc_g, :], op0=OP.mult, op1=OP.add),
                            waits=[("pe", g2), ("dve", cb_dve[tc_g]),
                                   ("act", sg_act[tc_g])])
                        dfree[par] = ("dve", rv)
                        if e == NEXP - 1:
                            resid[tc_g] = ("dve", rv)
                    hfree = ("pe", p.cnt["pe"])
                slot_free[wg_s] = ("pe", wg_last)
                slot_free[wu_s] = ("pe", wu_last)
                slot_free[wd_s] = ("pe", wd_last)
                if e + 1 < NEXP:
                    for j in range(3):
                        s_i = kexp(e + 1, j)
                        rear = "(hc p) i -> p hc i" if j < 2 else "(ic p) o -> p ic o"
                        slot_ld[s_i] = wload(
                            big(FSLOT[s_i], 11264).rearrange("p (a b) -> p a b", a=(ND if j < 2 else NI)),
                            srcs(e + 1)[j].rearrange(rear, p=128), "w%d" % s_i,
                            waits=[slot_free[s_i]])
            free["PC"], free["PE2"] = gfree
            free["PD"], free["PF"] = ufree
            free["PA0"] = free["PA1"] = dfree[0]
            free["PB0"] = free["PB1"] = dfree[1]
            free["SIL0"] = silfree[0]
            return resid, p.cnt["pe"]

        xres_rdy = [("x", 0)] * NT
        last_resid = 0
        prev_store = None
        for s_i in range(n_seq):
            for j in range(4):
                lv = xdma(XRES[:, 4 * j:4 * j + 4, :],
                          dx[s_i].rearrange("(c p) d -> p c d", p=128)[:, 4 * j:4 * j + 4, :],
                          waits=([prev_store[j]] if prev_store else [])
                                + [("dve", last_resid)],
                          key="x%d" % j)
                for tc in range(4 * j, 4 * j + 4):
                    xres_rdy[tc] = ("x%d" % j, lv)

            tmarks = lambda mk: ([("t%d" % i, mk["t%d" % i]) for i in range(4)]
                                 + [("t2", mk["t2"]), ("dve", mk["dve"]),
                                    ("act", mk["act"])])

            m = p.marks()
            free["BV_ones"] = ("pe", m["pe"])
            y_tp, tp_tc, ln_end = emit_ln(xres_rdy, m["pe"], tmarks(m))
            ms = emit_proj(wka, wva, wqa, y_tp, CONST[:, 2:3])
            xres_rdy = emit_attn("a", ms, woa, ln_end, m["dve"])

            if only_attn_a:
                last_resid = max(v for _, v in xres_rdy)
                for j in range(4):
                    xdma(dout[s_i].rearrange("(c p) d -> p c d", p=128)[:, 4 * j:4 * j + 4, :],
                         XRES[:, 4 * j:4 * j + 4, :],
                         waits=[xres_rdy[4 * j], xres_rdy[4 * j + 1],
                                xres_rdy[4 * j + 2], xres_rdy[4 * j + 3]])
                prev_store = p.cnt["x"]
                continue
            m = p.marks()
            free["SIL0"] = ("pe", m["pe"])
            y_tp, tp_tc, ln_end = emit_ln(xres_rdy, m["pe"], tmarks(m))
            xres_rdy, ffn_pe = emit_ffn(y_tp, tp_tc, m["pe"])

            m = p.marks()
            free["WA"] = ("pe", m["pe"])
            free["WB"] = ("pe", m["pe"])
            free["BV_ones"] = ("pe", m["pe"])
            y_tp, tp_tc, ln_end = emit_ln(xres_rdy, m["pe"], tmarks(m))
            ms = emit_proj(wkb, wvb, wqb, y_tp, CONST[:, 3:4])
            xres_rdy = emit_attn("b", ms, None, ln_end, m["dve"])

            m = p.marks()
            free["SIL0"] = ("pe", m["pe"])
            y_tp, tp_tc, ln_end = emit_ln(xres_rdy, m["pe"], tmarks(m))
            xres_rdy, ffn_pe = emit_ffn(y_tp, tp_tc, m["pe"])
            m = p.marks()
            free["WA"] = ("pe", m["pe"])
            free["WB"] = ("pe", m["pe"])
            last_resid = max(v for _, v in xres_rdy)

            store_of = []
            for j in range(4):
                sv = xdma(dout[s_i].rearrange("(c p) d -> p c d", p=128)[:, 4 * j:4 * j + 4, :],
                          XRES[:, 4 * j:4 * j + 4, :],
                          waits=[xres_rdy[4 * j], xres_rdy[4 * j + 1],
                                 xres_rdy[4 * j + 2], xres_rdy[4 * j + 3]],
                          key="x%d" % j)
                store_of.append(("x%d" % j, sv))
            prev_store = store_of

        for j in range(4):
            p.wait("sync", "x%d" % j, p.cnt["x%d" % j])
            p.wait("scalar", "x%d" % j, p.cnt["x%d" % j])

        with nc.Block() as block:
            def make_section(eng):
                def section(h):
                    for op in p.ops[eng]:
                        if op[0] == "w":
                            h.wait_ge(sems[op[1]], op[2])
                        else:
                            _, fn, inck, by = op
                            ins = fn(h)
                            if inck is not None:
                                ins.then_inc(sems[inck], by)
                return section
            for eng in ENGS:
                getattr(block, eng)(make_section(eng))

    return nc


_NC_CACHE = {}


def kernel(L0_attn_a_x, L0_attn_a_Wq, L0_attn_a_Wk, L0_attn_a_Wv, L0_attn_a_Wo,
           L0_attn_a_bo, L0_attn_a_denom_bias, L0_attn_b_Wq, L0_attn_b_Wk,
           L0_attn_b_Wv, L0_ffn_gate_w, L0_ffn_w_gate, L0_ffn_w_up,
           L0_ffn_w_down, L0_ffn_sw_gate, L0_ffn_sw_up, L0_ffn_sw_down,
           L0_ffn_shared_gate_w):
    x = np.asarray(L0_attn_a_x)
    tr = lambda w: np.ascontiguousarray(np.asarray(w).T)
    gw16 = np.zeros((16, D), np.float32)
    gw16[0:8] = np.asarray(L0_ffn_gate_w)
    gw16[8] = np.asarray(L0_ffn_shared_gate_w)[0]
    mask = np.triu(np.full((128, 128), -1e30, np.float32), k=1)

    common = {
        "wqa": tr(L0_attn_a_Wq), "wka": tr(L0_attn_a_Wk),
        "wva": tr(L0_attn_a_Wv), "woa": tr(L0_attn_a_Wo),
        "wqb": tr(L0_attn_b_Wq), "wkb": tr(L0_attn_b_Wk),
        "wvb": tr(L0_attn_b_Wv),
        "bo": np.ascontiguousarray(np.asarray(L0_attn_a_bo).reshape(1, D)),
        "db": np.ascontiguousarray(
            np.asarray(L0_attn_a_denom_bias).reshape(1, HEADS).astype(np.float32)),
        "gwt": tr(gw16),
        "w_gate": np.ascontiguousarray(L0_ffn_w_gate),
        "w_up": np.ascontiguousarray(L0_ffn_w_up),
        "w_down": np.ascontiguousarray(L0_ffn_w_down),
        "sw_gate": np.ascontiguousarray(L0_ffn_sw_gate),
        "sw_up": np.ascontiguousarray(L0_ffn_sw_up),
        "sw_down": np.ascontiguousarray(L0_ffn_sw_down),
        "mask": mask,
        "maskt": np.ascontiguousarray(mask.T),
        "idn": np.eye(128, dtype=ml_dtypes.bfloat16),
    }
    in_maps = []
    for c in range(NCORES):
        m = dict(common)
        m["x"] = np.ascontiguousarray(x[c * NSEQ:(c + 1) * NSEQ])
        in_maps.append(m)

    if "nc" not in _NC_CACHE:
        _NC_CACHE["nc"] = build_nc()
    nc = _NC_CACHE["nc"]
    res = run_bass_kernel_spmd(nc, in_maps, core_ids=list(range(NCORES)))
    return np.concatenate([np.asarray(r["out"], np.float32) for r in res.results],
                          axis=0)


# version 26
# speedup vs baseline: 1.0967x; 1.0967x over previous
import contextlib
import numpy as np
import ml_dtypes
import concourse.bass as bass
import concourse.mybir as mybir
from concourse.bass_utils import run_bass_kernel_spmd
from concourse.alu_op_type import AluOpType as OP

F32 = mybir.dt.float32
BF16 = mybir.dt.bfloat16
AF = mybir.ActivationFunctionType
AX = mybir.AxisListType

NCORES = 8
NSEQ = 4
S = 2048
D = 1024
NT = 16
ND = 8
HEADS = 16
HD = 64
NEXP = 9
DI = 1408
NI = 11
FMAX = float(np.finfo(np.float32).max)

ENGS = ("tensor", "scalar", "vector", "sync", "gpsimd")
SEMK = ("pe", "act", "dve", "t0", "t1", "t2", "t3", "w0", "w1", "w2", "w3", "wc", "x", "x0", "x1", "x2", "x3")

BK = 0
BV = 16384
BWA = 33024
BWB = 41216
BOF = 49408
BIG_SZ = 50688
FSLOT = [0, 11264, 22528, 33792]
BH = 45056


class Prog:
    def __init__(self):
        self.ops = {e: [] for e in ENGS}
        self.cnt = {k: 0 for k in SEMK}
        self.waited = {e: {k: 0 for k in SEMK} for e in ENGS}

    def wait(self, eng, key, val):
        if val is None or val <= 0:
            return
        if self.waited[eng][key] >= val:
            return
        self.waited[eng][key] = val
        self.ops[eng].append(("w", key, val))

    def emit(self, eng, fn, waits=(), inc=None, by=1):
        for k, v in waits:
            self.wait(eng, k, v)
        val = None
        if inc is not None:
            self.cnt[inc] += by
            val = self.cnt[inc]
        self.ops[eng].append(("i", fn, inc, by))
        return val

    def marks(self):
        return dict(self.cnt)


def ceil_div(a, b):
    return (a + b - 1) // b


def build_nc(n_seq=NSEQ, only_attn_a=False):
    nc = bass.Bass("TRN2", target_bir_lowering=False)

    dx = nc.dram_tensor("x", [n_seq, S, D], F32, kind="ExternalInput").ap()
    dout = nc.dram_tensor("out", [n_seq, S, D], F32, kind="ExternalOutput").ap()
    wqa = nc.dram_tensor("wqa", [D, D], F32, kind="ExternalInput").ap()
    wka = nc.dram_tensor("wka", [D, D], F32, kind="ExternalInput").ap()
    wva = nc.dram_tensor("wva", [D, D], F32, kind="ExternalInput").ap()
    woa = nc.dram_tensor("woa", [D, D], F32, kind="ExternalInput").ap()
    wqb = nc.dram_tensor("wqb", [D, D], F32, kind="ExternalInput").ap()
    wkb = nc.dram_tensor("wkb", [D, D], F32, kind="ExternalInput").ap()
    wvb = nc.dram_tensor("wvb", [D, D], F32, kind="ExternalInput").ap()
    dbo = nc.dram_tensor("bo", [1, D], F32, kind="ExternalInput").ap()
    ddb = nc.dram_tensor("db", [1, HEADS], F32, kind="ExternalInput").ap()
    dgw = nc.dram_tensor("gwt", [D, 16], F32, kind="ExternalInput").ap()
    dwg = nc.dram_tensor("w_gate", [8, D, DI], F32, kind="ExternalInput").ap()
    dwu = nc.dram_tensor("w_up", [8, D, DI], F32, kind="ExternalInput").ap()
    dwd = nc.dram_tensor("w_down", [8, DI, D], F32, kind="ExternalInput").ap()
    dsg = nc.dram_tensor("sw_gate", [D, DI], F32, kind="ExternalInput").ap()
    dsu = nc.dram_tensor("sw_up", [D, DI], F32, kind="ExternalInput").ap()
    dsd = nc.dram_tensor("sw_down", [DI, D], F32, kind="ExternalInput").ap()
    dmask = nc.dram_tensor("mask", [128, 128], F32, kind="ExternalInput").ap()
    dmaskt = nc.dram_tensor("maskt", [128, 128], F32, kind="ExternalInput").ap()
    didn = nc.dram_tensor("idn", [128, 128], BF16, kind="ExternalInput").ap()

    p = Prog()

    with contextlib.ExitStack() as st:
        sb = lambda name, shape, dt: st.enter_context(nc.sbuf_tensor(name, shape, dt))
        psumt = lambda name, shape: st.enter_context(nc.psum_tensor(name, shape, F32))

        XRES = sb("XRES", [128, NT, D], F32)
        Y = sb("Y", [128, ND, S], BF16)
        BIG = sb("BIG", [128, BIG_SZ], BF16)
        PT_ = sb("PT", [128, 2, 512], BF16)
        OUTTM = sb("OUTTM", [128, D], BF16)
        SIL = sb("SIL", [128, 512], F32)
        LNS = sb("LNS", [128, 4, NT], F32)
        AT = sb("AT", [128, 64], F32)
        NEGM = sb("NEGM", [128, 16], F32)
        U16 = sb("U16", [128, 16], F32)
        T16 = sb("T16", [128, 16], F32)
        SM = sb("SM", [128, 8], F32)
        SM2 = sb("SM2", [128, 8], F32)
        RT = sb("RT", [128, 40], F32)
        CB = sb("CB", [128, NT, 8], F32)
        CBSG = sb("CBSG", [128, NT], F32)
        MASKSB = sb("MASKSB", [128, 128], F32)
        MASKT = sb("MASKT", [128, 128], F32)
        ONESB = sb("ONESB", [128, 1], BF16)
        BOROW = sb("BOROW", [1, D], BF16)
        DBROW = sb("DBROW", [1, HEADS], F32)
        EDBROW = sb("EDBROW", [1, HEADS], F32)
        EDBB = sb("EDBB", [128, HEADS], F32)
        GWT = sb("GWT", [128, ND, 16], BF16)
        ONES = sb("ONES", [1, 128], BF16)
        ONESF = sb("ONESF", [1, 128], F32)
        CONST = sb("CONST", [128, 8], F32)

        PA = psumt("PA", [128, 1024])
        PBp = psumt("PBp", [128, 1024])
        PCD = psumt("PCD", [128, 1024])
        PC = PCD[:, 0:512]
        PD = PCD[:, 512:1024]
        PE2 = psumt("PE2", [128, 512])
        PF = psumt("PF", [128, 512])

        sems = {k: st.enter_context(nc.semaphore("s_" + k)) for k in SEMK}

        QBANKS = [PA[:, 0:512], PA[:, 512:1024], PBp[:, 0:512], PBp[:, 512:1024],
                  PC[:, 0:512], PD[:, 0:512], PE2[:, 0:512], PF[:, 0:512]]
        SBANKS = [PC, PD]

        big = lambda off, ln: BIG[:, off:off + ln]

        def mm(out_ap, lhsT, rhs, start, stop, waits=(), inc=None):
            return p.emit("tensor",
                          lambda h, o=out_ap, l=lhsT, r=rhs, s1=start, s2=stop:
                          h.matmul(o, l, r, start=s1, stop=s2),
                          waits=waits, inc=inc)

        def act(fn, waits=(), inc="act"):
            w = list(waits) + [("act", p.cnt["act"])]
            return p.emit("scalar", fn, waits=w, inc=inc)

        def dve(fn, waits=(), inc="dve"):
            w = list(waits) + [("dve", p.cnt["dve"])]
            return p.emit("vector", fn, waits=w, inc=inc)

        def tp(dst, src, key, waits=()):
            return p.emit("sync",
                          lambda h, d=dst, s=src: h.dma_start_transpose(d, s),
                          waits=waits, inc=key, by=16)

        def wload(dst, src, key, waits=()):
            return p.emit("gpsimd",
                          lambda h, d=dst, s=src: h.dma_start(out=d, in_=s),
                          waits=waits, inc=key, by=16)

        def xdma(dst, src, waits=(), key="x"):
            w = list(waits) + [("act", p.cnt["act"])]
            return p.emit("scalar",
                          lambda h, d=dst, s=src: h.dma_start(out=d, in_=s),
                          waits=w, inc=key, by=16)

        wload(MASKSB[:, 0:64].bitcast(BF16), dmask, "wc")
        wload(MASKT[:, 0:64].bitcast(BF16), dmaskt, "wc")
        wload(BOROW[:], dbo, "wc")
        wload(DBROW[:], ddb, "wc")
        wload(GWT[:], dgw.rearrange("(dc p) e -> p dc e", p=128), "wc")
        w_const = p.cnt["wc"]
        dve(lambda h: h.memset(ONES[:], 1.0))
        dve(lambda h: h.memset(ONESF[:], 1.0))
        dve(lambda h: h.memset(CONST[:, 0:1], 1e-5))
        dve(lambda h: h.memset(CONST[:, 1:2], -0.5))
        dve(lambda h: h.memset(CONST[:, 2:3], 0.125))
        dve(lambda h: h.memset(CONST[:, 3:4], 1.0 / 32.0))
        dve(lambda h: h.memset(CONST[:, 4:5], -30.0))
        dve(lambda h: h.memset(CONST[:, 5:6], -1.0))
        dve(lambda h: h.memset(ONESB[:], 1.0))
        ones_ready = p.cnt["dve"]
        edb_v = act(lambda h: h.activation(out=EDBROW[:], in_=DBROW[:], func=AF.Exp),
                    waits=[("wc", w_const)])
        mm(PC[:, 0:16], ONESF[:], EDBROW[:], True, True,
           waits=[("dve", ones_ready), ("act", edb_v)], inc="pe")
        pe_edb = p.cnt["pe"]
        act(lambda h: h.copy(out=EDBB[:], in_=PC[:, 0:16]), waits=[("pe", pe_edb)])
        edbb_ready = p.cnt["act"]

        state = {
            "xres_dve": 0,
            "x_load": 0,
            "pc_free": ("act", edbb_ready),
        }

        free = {
            "PA0": ("pe", 0), "PA1": ("pe", 0), "PB0": ("pe", 0), "PB1": ("pe", 0),
            "PC": ("act", edbb_ready), "PD": ("pe", 0), "PE2": ("pe", 0), "PF": ("pe", 0),
            "PTS0": ("pe", 0), "PTS1": ("pe", 0), "PTS2": ("pe", 0), "PTS3": ("pe", 0),
            "OUTTM": ("pe", 0), "SIL0": ("pe", 0), "SIL1": ("pe", 0),
            "Y": ("pe", 0), "BIG": ("pe", 0), "XRES_w": ("x", 0),
            "WA": ("pe", 0), "WB": ("pe", 0), "OF": ("pe", 0),
            "SLOT0": ("pe", 0), "SLOT1": ("pe", 0), "SLOT2": ("pe", 0), "SLOT3": ("pe", 0),
            "H": ("pe", 0),
        }

        def emit_ln(xres_rdy, y_free_pe, m_tp):
            yflat = Y[:].rearrange("p a b -> p (a b)")
            sq_scr = yflat[:, 15 * 1024:16 * 1024]
            ysc = [OUTTM[:], PT_[:].rearrange("p a b -> p (a b)")]
            for tc in range(NT):
                act(lambda h, tc=tc, o=sq_scr: h.activation(
                    out=o, in_=XRES[:, tc, :], func=AF.Square,
                    accum_out=LNS[:, 1, tc:tc + 1]),
                    waits=[xres_rdy[tc], ("pe", y_free_pe)])
                dve(lambda h, tc=tc, o=yflat[:, 14 * 1024:15 * 1024]: h.tensor_scalar(
                    out=o, in0=XRES[:, tc, :], scalar1=0.0, scalar2=0.0,
                    op0=OP.add, op1=OP.add, accum_out=LNS[:, 0, tc:tc + 1]),
                    waits=[xres_rdy[tc], ("pe", y_free_pe)])
            sumsq_done = p.cnt["act"]
            dve(lambda h: h.tensor_scalar(out=LNS[:, 2, :], in0=LNS[:, 0, :],
                                          scalar1=-1.0 / D, scalar2=None, op0=OP.mult))
            dve(lambda h: h.tensor_tensor(out=LNS[:, 0, :], in0=LNS[:, 2, :],
                                          in1=LNS[:, 2, :], op=OP.mult))
            var_v = dve(lambda h: h.scalar_tensor_tensor(
                out=LNS[:, 1, :], in0=LNS[:, 1, :], scalar=1.0 / D,
                in1=LNS[:, 0, :], op0=OP.mult, op1=OP.subtract),
                waits=[("act", sumsq_done)])
            lnv = act(lambda h: h.activation(out=LNS[:, 0, :], in_=LNS[:, 1, :],
                                             func=AF.Ln, bias=CONST[:, 0:1]),
                      waits=[("dve", var_v), ("dve", ones_ready)])
            rstd_v = act(lambda h: h.activation(out=LNS[:, 3, :], in_=LNS[:, 0, :],
                                                func=AF.Exp, scale=CONST[:, 1:2]))
            y_tp = []
            tp_of_tc = []
            IDN = big(50432, 128)
            idn_ld = wload(IDN, didn, "wc",
                           waits=[("pe", y_free_pe)] + list(m_tp))
            tp_bank = lambda i: (PA if i < 16 else PBp)[:, (i % 16) * 64:(i % 16) * 64 + 64].bitcast(BF16)
            evac_hist = []
            for tc in range(NT):
                scr = ysc[tc % 2]
                w_tp = [tp_of_tc[tc - 2]] if tc >= 2 else m_tp
                nv = dve(lambda h, tc=tc, o=scr: h.tensor_scalar(
                    out=o, in0=XRES[:, tc, :],
                    scalar1=LNS[:, 2, tc:tc + 1], scalar2=LNS[:, 3, tc:tc + 1],
                    op0=OP.add, op1=OP.mult),
                    waits=[("act", rstd_v), ("pe", y_free_pe)] + list(w_tp))
                last_tp = None
                for dc in range(ND):
                    ti = (tc % 2) * 8 + dc
                    wv = [("dve", nv), ("wc", idn_ld)]
                    if tc < 2:
                        wv += list(m_tp)
                    if len(evac_hist) >= 2:
                        wv.append(("act", evac_hist[-2]))
                    last_tp = p.emit("tensor",
                                     lambda h, o=tp_bank(ti),
                                     i_=scr[:, dc * 128:(dc + 1) * 128], idn=IDN:
                                     h.transpose(o, i_, idn),
                                     waits=wv, inc="pe")
                ev = None
                for dc in range(ND):
                    ti = (tc % 2) * 8 + dc
                    ev = act(lambda h, tc=tc, dc=dc, i_=tp_bank(ti): h.copy(
                        out=Y[:, dc, tc * 128:(tc + 1) * 128], in_=i_),
                        waits=[("pe", last_tp)])
                evac_hist.append(ev)
                tp_of_tc.append(("act", ev))
            for tt in range(4):
                y_tp.append([tp_of_tc[tt * 4 + i] for i in range(4)])
            for nm in ("PA0", "PA1", "PB0", "PB1"):
                free[nm] = ("act", p.cnt["act"])
            ln_end = [("act", p.cnt["act"])]
            return y_tp, tp_of_tc, ln_end

        def emit_proj(wk_d, wv_d, wq_d, y_tp, qscale_ap):
            wa_ld = wload(big(BWA, 8192).rearrange("p (dc o) -> p dc o", o=D),
                          wk_d.rearrange("(dc p) o -> p dc o", p=128), "w0",
                          waits=[free["WA"]])
            wb_ld = wload(big(BWB, 8192).rearrange("p (dc o) -> p dc o", o=D),
                          wv_d.rearrange("(dc p) o -> p dc o", p=128), "w1",
                          waits=[free["WB"]])
            dve(lambda h: h.memset(
                bass.AP(BIG, BV + 64, [[BIG_SZ, 128], [1040, 16], [65, 16]]), 1.0),
                waits=[free["BV_ones"]] if "BV_ones" in free else [])
            ones_v = p.cnt["dve"]
            kbank_free = [("pe", 0)] * 4
            kv = 0
            for tt in range(4):
                for oc in range(ND):
                    bidx = (tt * ND + oc) % 4
                    bank = [PC, PD, PE2, PF][bidx]
                    waits = [("w0", wa_ld), kbank_free[bidx],
                             free[["PC", "PD", "PE2", "PF"][bidx]]] + list(y_tp[tt])
                    for dc in range(ND):
                        g = mm(bank[:, 0:512],
                               big(BWA, 8192)[:, dc * D + oc * 128: dc * D + (oc + 1) * 128],
                               Y[:, dc, tt * 512:(tt + 1) * 512],
                               dc == 0, dc == ND - 1,
                               waits=waits if dc == 0 else (),
                               inc="pe" if dc == ND - 1 else None)
                    kv = act(lambda h, oc=oc, tt=tt, b=bank: h.copy(
                        out=big(BK, 16384).rearrange("p (c t) -> p c t", c=ND)[:, oc, tt * 512:(tt + 1) * 512],
                        in_=b[:, 0:512]), waits=[("pe", g)])
                    kbank_free[bidx] = ("act", kv)
            k_done = kv
            kproj_pe = p.cnt["pe"]
            vfree = [free["PA0"], free["PA1"], free["PB0"], free["PB1"]]
            vv = 0
            for tc in range(NT):
                bank = PA if tc % 2 == 0 else PBp
                bw = [vfree[0], vfree[1]] if tc % 2 == 0 else [vfree[2], vfree[3]]
                gl = None
                for half in range(2):
                    for dc in range(ND):
                        gl = mm(bank[:, half * 512:(half + 1) * 512],
                                Y[:, dc, tc * 128:(tc + 1) * 128],
                                big(BWB, 8192)[:, dc * D + half * 512: dc * D + (half + 1) * 512],
                                dc == 0, dc == ND - 1,
                                waits=([("w1", wb_ld), bw[half]] if dc == 0 else ()),
                                inc="pe" if dc == ND - 1 else None)
                vv = act(lambda h, tc=tc, b=bank: h.copy(
                    out=bass.AP(BIG, BV + tc * 1040, [[BIG_SZ, 128], [65, 16], [1, 64]]),
                    in_=b[:]), waits=[("pe", gl)])
                if tc % 2 == 0:
                    vfree[0] = vfree[1] = ("act", vv)
                else:
                    vfree[2] = vfree[3] = ("act", vv)
            v_done = vv
            vproj_pe = p.cnt["pe"]
            wq_ld = wload(big(BWA, 8192).rearrange("p (dc o) -> p dc o", o=D),
                          wq_d.rearrange("(dc p) o -> p dc o", p=128), "w0",
                          waits=[("pe", kproj_pe)])
            evac_of_bank = [("act", 0)] * 8
            for tt in range(4):
                last_g = None
                for oc in range(ND):
                    bank = QBANKS[oc]
                    waits = [("w0", wq_ld), evac_of_bank[oc]]
                    if tt == 0:
                        waits.append(("act", v_done))
                        waits.append(("act", k_done))
                    for dc in range(ND):
                        last_g = mm(bank, big(BWA, 8192)[:, dc * D + oc * 128: dc * D + (oc + 1) * 128],
                                    Y[:, dc, tt * 512:(tt + 1) * 512],
                                    dc == 0, dc == ND - 1,
                                    waits=waits if dc == 0 else (),
                                    inc="pe" if (dc == ND - 1 and oc == ND - 1) else None)
                for oc in range(ND):
                    ev = act(lambda h, oc=oc, tt=tt, b=QBANKS[oc]: h.activation(
                        out=Y[:, oc, tt * 512:(tt + 1) * 512], in_=b,
                        func=AF.Copy, scale=qscale_ap),
                        waits=[("pe", last_g), ("dve", ones_ready)])
                    evac_of_bank[oc] = ("act", ev)
            q_done = p.cnt["act"]
            for oc, nm in enumerate(["PA0", "PA1", "PB0", "PB1",
                                     "PC", "PD", "PE2", "PF"]):
                free[nm] = evac_of_bank[oc]
            return dict(k_done=k_done, v_done=v_done, q_done=q_done,
                        vproj_pe=vproj_pe, kproj_pe=kproj_pe,
                        qproj_pe=p.cnt["pe"], ones_v=ones_v)

        PTF = PT_[:].rearrange("p a b -> p (a b)")

        def pt_slot(c, par, lo, ln):
            if par == 0:
                return PTF[:, c * 512 + lo: c * 512 + lo + ln]
            return SIL[:, (c * 512 + lo) // 2: (c * 512 + lo + ln) // 2].bitcast(BF16)

        def emit_attn(kind, ms, wo_d, ln_end_tp, m_dve=0):
            kview = big(BK, 16384).rearrange("p (c t) -> p c t", c=ND)
            v65 = lambda kc, h_i: BIG[:, BV + kc * 1040 + h_i * 65: BV + kc * 1040 + h_i * 65 + 65]
            vb_half = lambda kc, half: bass.AP(
                BIG, BV + kc * 1040 + half * 8 * 65,
                [[BIG_SZ, 128], [65, 8], [1, 64]])
            vb_one = lambda kc: BIG[:, BV + kc * 1040 + 64: BV + kc * 1040 + 65]
            qk_w = [("act", ms["q_done"]), ("act", ms["k_done"])]
            resid = []
            if kind == "a":
                wo_ld = wload(big(BWB, 8192).rearrange("p (dc o) -> p dc o", o=D),
                              wo_d.rearrange("(dc p) o -> p dc o", p=128), "w1",
                              waits=[("pe", ms["vproj_pe"])])
                sfree = {(0, 0): [free["PC"]], (0, 1): [free["PA0"]],
                         (1, 0): [free["PD"]], (1, 1): [free["PA1"]]}
                hfree = [free["PB0"], free["PB1"]]
                ptfree = {(0, 0): [free["PTS0"], free["PTS1"]] + list(ln_end_tp),
                          (1, 0): [free["PTS2"], free["PTS3"]] + list(ln_end_tp),
                          (0, 1): [("dve", m_dve)],
                          (1, 1): [("dve", m_dve)]}
                accfree = [[free["PE2"]], [free["PF"]]]
                pafree = [[free["PA0"]], [free["PA1"]]]
                outtm_free = list(ln_end_tp)
                slot_i = 0
                pend_op = None

                def emit_oproj(qc_, of_tp_):
                    zg = None
                    for half in range(2):
                        for dc in range(ND):
                            mm(PA[:, half * 512:(half + 1) * 512],
                               big(BOF, 1024)[:, dc * 128:(dc + 1) * 128],
                               big(BWB, 8192)[:, dc * D + half * 512: dc * D + (half + 1) * 512],
                               dc == 0, False,
                               waits=(([("t2", of_tp_), ("w1", wo_ld),
                                        ("dve", ones_ready), ("wc", w_const)]
                                       + pafree[half]
                                       + sfree[(half, 1)])
                                      if dc == 0 else ()))
                        zg = mm(PA[:, half * 512:(half + 1) * 512], ONES[:],
                                BOROW[:, half * 512:(half + 1) * 512],
                                False, True, inc="pe")
                    free["OF"] = ("pe", p.cnt["pe"])
                    rv = dve(lambda h, qc_=qc_: h.tensor_tensor(
                        out=XRES[:, qc_, :], in0=XRES[:, qc_, :], in1=PA[:],
                        op=OP.add), waits=[("pe", zg)])
                    pafree[0] = pafree[1] = [("dve", rv)]
                    sfree[(0, 1)] = [("dve", rv)]
                    sfree[(1, 1)] = [("dve", rv)]
                    resid.append(("dve", rv))

                for qc in range(NT):
                    klen = (qc + 1) * 128
                    nkt = ceil_div(klen, 512)
                    widths = [min(512, klen - 512 * j) for j in range(nkt)]
                    nck = qc + 1
                    for hgp in range(2):
                        hgs = (2 * hgp, 2 * hgp + 1)
                        ph1_banks = [PBp[:, 0:512], PBp[:, 512:1024], PC, PD]
                        ph1_pairs = [PBp[:], PCD[:]]
                        ph1_free = [hfree[0], hfree[1],
                                    sfree[(0, 0)][0], sfree[(1, 0)][0]]
                        bctr = 0
                        for hg in hgs:
                            for j4 in range(4):
                                h_i = hg * 4 + j4
                                pbase, oc_h = 64 * (h_i % 2), h_i // 2
                                q_ap = Y[pbase:pbase + 64, oc_h, qc * 128:(qc + 1) * 128]
                                if nkt <= 2:
                                    pr = h_i % 2
                                    pair = ph1_pairs[pr]
                                    g = None
                                    for kt in range(nkt):
                                        w_ = widths[kt]
                                        g = mm(pair[:, kt * 512:kt * 512 + w_],
                                               q_ap,
                                               kview[pbase:pbase + 64, oc_h, kt * 512:kt * 512 + w_],
                                               True, kt != nkt - 1,
                                               waits=qk_w + [ph1_free[pr * 2],
                                                             ph1_free[pr * 2 + 1]]
                                                     if kt == 0 else qk_w,
                                               inc="pe" if kt == nkt - 1 else None)
                                    gm = mm(pair[:, klen - 128:klen],
                                            big(50432, 128),
                                            MASKSB[:, 0:64].bitcast(BF16),
                                            False, True,
                                            waits=[("pe", g), ("wc", w_const)],
                                            inc="pe")
                                    mx = dve(lambda h, h_i=h_i, b=pair, w_=klen:
                                             h.tensor_reduce(
                                                 out=NEGM[:, h_i:h_i + 1],
                                                 in_=b[:, 0:w_], axis=AX.X,
                                                 op=OP.max, negate=True),
                                             waits=[("pe", gm)])
                                    ph1_free[pr * 2] = ("dve", mx)
                                    ph1_free[pr * 2 + 1] = ("dve", mx)
                                    continue
                                for kt in range(nkt):
                                    w_ = widths[kt]
                                    bsel = bctr % 4
                                    bctr += 1
                                    bank = ph1_banks[bsel]
                                    g = mm(bank[:, 0:w_], q_ap,
                                           kview[pbase:pbase + 64, oc_h, kt * 512:kt * 512 + w_],
                                           True, kt != nkt - 1,
                                           waits=qk_w + [ph1_free[bsel]], inc="pe")
                                    wl = [("pe", g)]
                                    if kt == nkt - 1:
                                        gm = mm(bank[:, w_ - 128:w_],
                                                big(50432, 128),
                                                MASKSB[:, 0:64].bitcast(BF16),
                                                False, True,
                                                waits=[("pe", g), ("wc", w_const)],
                                                inc="pe")
                                        wl = [("pe", gm)]
                                    mx = dve(lambda h, kt=kt, h_i=h_i, b=bank, w_=w_:
                                             h.tensor_reduce(
                                                 out=AT[:, h_i * 4 + kt:h_i * 4 + kt + 1],
                                                 in_=b[:, 0:w_], axis=AX.X, op=OP.max),
                                             waits=wl)
                                    ph1_free[bsel] = ("dve", mx)
                                dve(lambda h, h_i=h_i, n=nkt: h.tensor_reduce(
                                    out=NEGM[:, h_i:h_i + 1],
                                    in_=AT[:, h_i * 4:h_i * 4 + n],
                                    axis=AX.X, op=OP.max, negate=True))
                        hfree[0], hfree[1] = ph1_free[0], ph1_free[1]
                        sfree[(0, 0)] = [ph1_free[2]]
                        sfree[(1, 0)] = [ph1_free[3]]
                        negm8 = p.cnt["dve"]
                        uu = act(lambda h, hg=hgs[0]: h.activation(
                            out=U16[:, hg * 4:hg * 4 + 8],
                            in_=NEGM[:, hg * 4:hg * 4 + 8], func=AF.Exp,
                            scale=CONST[:, 5:6], bias=CONST[:, 4:5]),
                            waits=[("dve", negm8), ("dve", ones_ready)])
                        t16 = dve(lambda h, hg=hgs[0]: h.tensor_tensor(
                            out=T16[:, hg * 4:hg * 4 + 8],
                            in0=U16[:, hg * 4:hg * 4 + 8],
                            in1=EDBB[:, hg * 4:hg * 4 + 8], op=OP.mult),
                            waits=[("act", uu), ("act", edbb_ready)])
                        if hgp == 0 and pend_op is not None:
                            emit_oproj(*pend_op)
                            pend_op = None
                        jpos = {0: 0, 2: 128, 1: 256, 3: 384}
                        abank_of = {0: PE2, 1: PF}
                        pv_last = {0: None, 1: None}
                        pend_pv = None

                        def emit_pvs(job):
                            kcp, parp, exd = job
                            for c in (0, 1):
                                wpv = [("act", exd[c])]
                                if kcp == 0:
                                    wpv += [("act", ms["v_done"]),
                                            ("dve", ms["ones_v"])] + accfree[c]
                                for j in range(4):
                                    h_i = hgs[c] * 4 + j
                                    lhs = pt_slot(c, parp, jpos[j], 128)
                                    pv_last[c] = mm(
                                        abank_of[c][:, j * 128:j * 128 + 65], lhs,
                                        v65(kcp, h_i),
                                        kcp == 0 and j == 0,
                                        kcp == nck - 1 and j == 3,
                                        waits=wpv if j == 0 else (),
                                        inc="pe" if (kcp == nck - 1 and j == 3) else None)
                                ptfree[(c, parp)] = [("pe", pv_last[c])]

                        for kc in range(nck):
                            par = kc % 2
                            bank_of = {0: PC if par == 0 else PA[:, 0:512],
                                       1: PD if par == 0 else PA[:, 512:1024]}
                            mm_done = {}
                            for c in (0, 1):
                                bank = bank_of[c]
                                g0 = None
                                for j in (0, 2):
                                    h_i = hgs[c] * 4 + j
                                    oc_h = h_i // 2
                                    g0 = mm(bank[:, jpos[j]:jpos[j] + 128],
                                            kview[0:64, oc_h, kc * 128:(kc + 1) * 128],
                                            Y[0:64, oc_h, qc * 128:(qc + 1) * 128],
                                            j == 0, False,
                                            waits=list(sfree[(c, par)]) if j == 0 else (),
                                            inc="pe" if j == 2 else None)
                                g1 = None
                                for j in (1, 3):
                                    h_i = hgs[c] * 4 + j
                                    oc_h = h_i // 2
                                    g1 = mm(bank[:, jpos[j]:jpos[j] + 128],
                                            kview[64:128, oc_h, kc * 128:(kc + 1) * 128],
                                            Y[64:128, oc_h, qc * 128:(qc + 1) * 128],
                                            False, j == 3 and kc != qc,
                                            waits=[("pe", g0)] if j == 1 else (),
                                            inc="pe" if j == 3 else None)
                                mm_done[c] = ("pe", g1)
                            if kc == qc:
                                for c in (0, 1):
                                    gd = None
                                    for j in range(4):
                                        gd = mm(bank_of[c][:, jpos[j]:jpos[j] + 128],
                                                big(50432, 128),
                                                MASKT[:, 0:64].bitcast(BF16),
                                                False, j == 3,
                                                waits=[mm_done[c], ("wc", w_const)]
                                                      if j == 0 else (),
                                                inc="pe" if j == 3 else None)
                                    mm_done[c] = ("pe", gd)
                            src = PCD[:] if par == 0 else PA[:]
                            dst = (PTF[:, 0:1024] if par == 0
                                   else SIL[:, 0:512].bitcast(BF16))
                            ex = act(lambda h, b=src, o=dst: h.activation(
                                out=o, in_=b, func=AF.Exp, bias=CONST[:, 4:5]),
                                waits=[mm_done[0], mm_done[1], ("dve", ones_ready)]
                                      + list(ptfree[(0, par)]) + list(ptfree[(1, par)]))
                            sfree[(0, par)] = [("act", ex)]
                            sfree[(1, par)] = [("act", ex)]
                            exd = {0: ex, 1: ex}
                            if pend_pv is not None:
                                emit_pvs(pend_pv)
                            pend_pv = (kc, par, exd)
                        emit_pvs(pend_pv)
                        ot = None
                        for c in (0, 1):
                            for j in range(4):
                                h_i = hgs[c] * 4 + j
                                dve(lambda h, h_i=h_i, j=j, c=c, a=abank_of[c]:
                                    h.tensor_tensor(
                                        out=SM[:, c * 4 + j:c * 4 + j + 1],
                                        in0=T16[:, h_i:h_i + 1],
                                        in1=a[:, j * 128 + 64:j * 128 + 65],
                                        op=OP.add),
                                    waits=[("dve", t16), ("pe", pv_last[c])])
                        rec = dve(lambda h: h.reciprocal(out=SM2[:], in_=SM[:]))
                        for c in (0, 1):
                            for j in range(4):
                                h_i = hgs[c] * 4 + j
                                ot = act(lambda h, h_i=h_i, j=j, c=c, a=abank_of[c]:
                                         h.activation(
                                             out=OUTTM[:, 64 * h_i:64 * h_i + 64],
                                             in_=a[:, j * 128:j * 128 + 64],
                                             func=AF.Copy,
                                             scale=SM2[:, c * 4 + j:c * 4 + j + 1]),
                                         waits=[("dve", rec)] + list(outtm_free))
                            accfree[c] = [("act", ot)]
                    heads_done = p.cnt["act"]
                    for dc in range(ND):
                        tp(big(BOF, 1024)[:, dc * 128:(dc + 1) * 128],
                           OUTTM[:, dc * 128:(dc + 1) * 128], "t2",
                           waits=[("act", heads_done), free["OF"]])
                    of_tp = p.cnt["t2"]
                    outtm_free = [("t2", of_tp)]
                    pend_op = (qc, of_tp)
                if pend_op is not None:
                    emit_oproj(*pend_op)
                free["PC"] = sfree[(0, 0)][0]
                free["PD"] = sfree[(1, 0)][0]
                free["PB0"] = hfree[0]
                free["PB1"] = hfree[1]
                free["PE2"] = accfree[0][0]
                free["PF"] = accfree[1][0]
                free["PA0"] = pafree[0][0]
                free["PA1"] = pafree[1][0]
                free["PTS0"] = free["PTS1"] = ptfree[(0, 0)][0]
                free["PTS2"] = free["PTS3"] = ptfree[(1, 0)][0]
            else:
                SB4 = [PC[:, 0:128], PD[:, 0:128],
                       PBp[:, 0:128], PBp[:, 512:640]]
                sfree = [free["PC"], free["PD"], free["PB0"], free["PB1"]]
                ptfree = [[free["PTS%d" % (i % 4)]] + list(ln_end_tp)
                          if i < 4 else [("dve", m_dve)] for i in range(8)]
                pffree = [free["PF"]]
                pafree = [free["PA0"], free["PA1"]]
                for qc in range(NT):
                    nck = qc + 1
                    pv_g = None
                    for kc in range(nck):
                        bank = SB4[kc % 4]
                        g = None
                        for dc in range(ND):
                            g = mm(bank,
                                   kview[:, dc, kc * 128:(kc + 1) * 128],
                                   Y[:, dc, qc * 128:(qc + 1) * 128],
                                   dc == 0, dc == ND - 1 and kc != qc,
                                   waits=([sfree[kc % 4]] + qk_w if dc == 0 else ()),
                                   inc="pe" if (dc == ND - 1 and kc != qc) else None)
                        lastw = ("pe", g)
                        if kc == qc:
                            g = mm(bank, big(50432, 128),
                                   MASKT[:, 0:64].bitcast(BF16),
                                   False, True, waits=[("wc", w_const)],
                                   inc="pe")
                            lastw = ("pe", g)
                        slot = kc % 8
                        po = pt_slot(slot // 4, slot % 4 // 2, (slot % 2) * 256, 128) \
                            if False else (
                            PTF[:, (slot % 4) * 128:(slot % 4) * 128 + 128] if slot < 4
                            else SIL[:, (slot - 4) * 64:(slot - 4) * 64 + 64].bitcast(BF16))
                        ex = act(lambda h, b=bank, o=po: h.activation(
                            out=o, in_=b,
                            func=AF.Exp, bias=CONST[:, 4:5]),
                            waits=[lastw, ("dve", ones_ready)] + list(ptfree[slot]))
                        sfree[kc % 4] = ("act", ex)
                        wpv = [("act", ex)]
                        if kc == 0:
                            wpv += [pafree[0], pafree[1],
                                    ("act", ms["v_done"])] + list(pffree)
                        lhs = po
                        for half in range(2):
                            mm(PA[:, half * 512:(half + 1) * 512], lhs,
                               vb_half(kc, half),
                               kc == 0, kc == nck - 1,
                               waits=wpv if half == 0 else ())
                        pv_g = mm(PF[:, 0:1], lhs, ONESB[:],
                                  kc == 0, kc == nck - 1, inc="pe")
                        ptfree[slot] = [("pe", pv_g)]
                    rec = dve(lambda h: h.reciprocal(out=SM[:, 3:4], in_=PF[:, 0:1]),
                              waits=[("pe", pv_g)])
                    pffree = [("dve", rec)]
                    rv = dve(lambda h, qc=qc: h.scalar_tensor_tensor(
                        out=XRES[:, qc, :], in0=PA[:],
                        scalar=SM[:, 3:4], in1=XRES[:, qc, :],
                        op0=OP.mult, op1=OP.add))
                    pafree = [("dve", rv), ("dve", rv)]
                    resid.append(("dve", rv))
                free["PC"] = sfree[0]
                free["PD"] = sfree[1]
                free["PB0"] = sfree[2]
                free["PB1"] = sfree[3]
                free["PA0"] = pafree[0]
                free["PA1"] = pafree[1]
                free["PF"] = pffree[0]
                for i in range(4):
                    free["PTS%d" % i] = ptfree[i][0]
                free["SILB"] = ptfree[4][0]
            return resid

        def emit_ffn(y_tp, tp_tc, slot_wait_pe):
            kexp = lambda e, j: (3 * e + j) % 4
            slot_ld = {}
            srcs = lambda e: [(dwg[e] if e < 8 else dsg),
                              (dwu[e] if e < 8 else dsu),
                              (dwd[e] if e < 8 else dsd)]
            for j in range(3):
                s_i = kexp(0, j)
                rear = "(hc p) i -> p hc i" if j < 2 else "(ic p) o -> p ic o"
                slot_ld[s_i] = wload(
                    big(FSLOT[s_i], 11264).rearrange("p (a b) -> p a b", a=(ND if j < 2 else NI)),
                    srcs(0)[j].rearrange(rear, p=128), "w%d" % s_i,
                    waits=[("pe", slot_wait_pe)])
            slot_free = {i: ("pe", slot_wait_pe) for i in range(4)}

            pc_free = free["PC"]
            for tc in range(NT):
                g = None
                for dc in range(ND):
                    g = mm(PC[:, 0:16], Y[:, dc, tc * 128:(tc + 1) * 128],
                           GWT[:, dc, :], dc == 0, dc == ND - 1,
                           waits=([("wc", w_const), tp_tc[tc], pc_free]
                                  if dc == 0 else ()),
                           inc="pe" if dc == ND - 1 else None)
                nm = dve(lambda h: h.tensor_reduce(
                    out=RT[:, 32:33], in_=PC[:, 0:8], axis=AX.X, op=OP.max,
                    negate=True), waits=[("pe", g)])
                ex = act(lambda h: h.activation(out=RT[:, 0:8], in_=PC[:, 0:8],
                                                func=AF.Exp, bias=RT[:, 32:33]),
                         waits=[("dve", nm)])
                sg = act(lambda h, tc=tc: h.activation(
                    out=CBSG[:, tc:tc + 1], in_=PC[:, 8:9], func=AF.Sigmoid))
                pc_free = ("act", sg)
                dve(lambda h: h.tensor_reduce(out=RT[:, 33:34], in_=RT[:, 0:8],
                                              axis=AX.X, op=OP.max),
                    waits=[("act", ex)])
                dve(lambda h: h.tensor_scalar(out=RT[:, 8:16], in0=RT[:, 0:8],
                                              scalar1=RT[:, 33:34], scalar2=None,
                                              op0=OP.is_ge))
                dve(lambda h: h.scalar_tensor_tensor(
                    out=RT[:, 16:24], in0=RT[:, 8:16], scalar=-1e30,
                    in1=RT[:, 0:8], op0=OP.mult, op1=OP.add))
                dve(lambda h: h.tensor_reduce(out=RT[:, 34:35], in_=RT[:, 16:24],
                                              axis=AX.X, op=OP.max))
                dve(lambda h: h.tensor_scalar(out=RT[:, 24:32], in0=RT[:, 16:24],
                                              scalar1=RT[:, 34:35], scalar2=None,
                                              op0=OP.is_ge))
                dve(lambda h: h.tensor_tensor(out=RT[:, 8:16], in0=RT[:, 8:16],
                                              in1=RT[:, 24:32], op=OP.add))
                dve(lambda h: h.tensor_tensor(out=RT[:, 33:34], in0=RT[:, 33:34],
                                              in1=RT[:, 34:35], op=OP.add))
                dve(lambda h: h.reciprocal(out=RT[:, 35:36], in_=RT[:, 33:34]))
                dve(lambda h, tc=tc: h.scalar_tensor_tensor(
                    out=CB[:, tc, :], in0=RT[:, 0:8], scalar=RT[:, 35:36],
                    in1=RT[:, 8:16], op0=OP.mult, op1=OP.mult))
            router_dve = p.cnt["dve"]
            router_act = p.cnt["act"]
            free["PC"] = pc_free

            gb = [PC, PE2]
            ub = [PD, PF]
            gfree = [free["PC"], free["PE2"]]
            ufree = [free["PD"], free["PF"]]
            dfree = [free["PA0"], free["PB0"]]
            silfree = [free["SIL0"]]
            hfree = ("pe", slot_wait_pe)
            resid = [None] * NT
            for e in range(NEXP):
                wg_s, wu_s, wd_s = (kexp(e, 0), kexp(e, 1), kexp(e, 2))
                wg_last = wu_last = wd_last = None
                for tt in range(4):
                    h_dve = []
                    for ic in range(NI):
                        par = ic % 2
                        g = None
                        for hc in range(ND):
                            g = mm(gb[par][:, 0:512],
                                   big(FSLOT[wg_s], 11264)[:, hc * DI + ic * 128: hc * DI + (ic + 1) * 128],
                                   Y[:, hc, tt * 512:(tt + 1) * 512],
                                   hc == 0, hc == ND - 1,
                                   waits=(([("w%d" % wg_s, slot_ld[wg_s]),
                                           gfree[par], hfree] + list(y_tp[tt])) if hc == 0 else ()),
                                   inc="pe" if hc == ND - 1 else None)
                        wg_last = g
                        u = None
                        for hc in range(ND):
                            u = mm(ub[par][:, 0:512],
                                   big(FSLOT[wu_s], 11264)[:, hc * DI + ic * 128: hc * DI + (ic + 1) * 128],
                                   Y[:, hc, tt * 512:(tt + 1) * 512],
                                   hc == 0, hc == ND - 1,
                                   waits=([("w%d" % wu_s, slot_ld[wu_s]), ufree[par]]
                                          if hc == 0 else ()),
                                   inc="pe" if hc == ND - 1 else None)
                        wu_last = u
                        sl = act(lambda h, par=par: h.activation(
                            out=SIL[:], in_=gb[par][:, 0:512],
                            func=AF.Silu), waits=[("pe", g), silfree[0]])
                        gfree[par] = ("act", sl)
                        hv = dve(lambda h, ic=ic, par=par: h.tensor_tensor(
                            out=big(BH, NI * 512)[:, ic * 512:(ic + 1) * 512],
                            in0=SIL[:], in1=ub[par][:, 0:512],
                            op=OP.mult),
                            waits=[("act", sl), ("pe", u), hfree])
                        ufree[par] = ("dve", hv)
                        silfree[0] = ("dve", hv)
                        h_dve.append(hv)
                    h_ready = h_dve[-1]
                    for tcl in range(4):
                        tc_g = tt * 4 + tcl
                        par = tc_g % 2
                        bank = PA if par == 0 else PBp
                        g2 = None
                        for half in range(2):
                            for ic in range(NI):
                                g2 = mm(bank[:, half * 512:(half + 1) * 512],
                                        big(BH, NI * 512)[:, ic * 512 + tcl * 128: ic * 512 + (tcl + 1) * 128],
                                        big(FSLOT[wd_s], 11264)[:, ic * D + half * 512: ic * D + (half + 1) * 512],
                                        ic == 0, ic == NI - 1,
                                        waits=([("dve", h_ready),
                                                ("w%d" % wd_s, slot_ld[wd_s]),
                                                dfree[par]] if ic == 0 else ()),
                                        inc="pe" if ic == NI - 1 else None)
                        wd_last = g2
                        scal = (CB[:, tc_g, e:e + 1] if e < 8
                                else CBSG[:, tc_g:tc_g + 1])
                        rv = dve(lambda h, tc_g=tc_g, b=bank, s=scal: h.scalar_tensor_tensor(
                            out=XRES[:, tc_g, :], in0=b[:], scalar=s,
                            in1=XRES[:, tc_g, :], op0=OP.mult, op1=OP.add),
                            waits=[("pe", g2), ("dve", router_dve),
                                   ("act", router_act)])
                        dfree[par] = ("dve", rv)
                        if e == NEXP - 1:
                            resid[tc_g] = ("dve", rv)
                    hfree = ("pe", p.cnt["pe"])
                slot_free[wg_s] = ("pe", wg_last)
                slot_free[wu_s] = ("pe", wu_last)
                slot_free[wd_s] = ("pe", wd_last)
                if e + 1 < NEXP:
                    for j in range(3):
                        s_i = kexp(e + 1, j)
                        rear = "(hc p) i -> p hc i" if j < 2 else "(ic p) o -> p ic o"
                        slot_ld[s_i] = wload(
                            big(FSLOT[s_i], 11264).rearrange("p (a b) -> p a b", a=(ND if j < 2 else NI)),
                            srcs(e + 1)[j].rearrange(rear, p=128), "w%d" % s_i,
                            waits=[slot_free[s_i]])
            free["PC"], free["PE2"] = gfree
            free["PD"], free["PF"] = ufree
            free["PA0"] = free["PA1"] = dfree[0]
            free["PB0"] = free["PB1"] = dfree[1]
            free["SIL0"] = silfree[0]
            return resid, p.cnt["pe"]

        xres_rdy = [("x", 0)] * NT
        last_resid = 0
        prev_store = None
        for s_i in range(n_seq):
            for j in range(4):
                lv = xdma(XRES[:, 4 * j:4 * j + 4, :],
                          dx[s_i].rearrange("(c p) d -> p c d", p=128)[:, 4 * j:4 * j + 4, :],
                          waits=([prev_store[j]] if prev_store else []),
                          key="x%d" % j)
                for tc in range(4 * j, 4 * j + 4):
                    xres_rdy[tc] = ("x%d" % j, lv)

            tmarks = lambda mk: ([("t%d" % i, mk["t%d" % i]) for i in range(4)]
                                 + [("t2", mk["t2"]), ("dve", mk["dve"]),
                                    ("act", mk["act"])])

            m = p.marks()
            free["BV_ones"] = ("pe", m["pe"])
            y_tp, tp_tc, ln_end = emit_ln(xres_rdy, m["pe"], tmarks(m))
            ms = emit_proj(wka, wva, wqa, y_tp, CONST[:, 2:3])
            xres_rdy = emit_attn("a", ms, woa, ln_end, m["dve"])

            if only_attn_a:
                last_resid = max(v for _, v in xres_rdy)
                for j in range(4):
                    xdma(dout[s_i].rearrange("(c p) d -> p c d", p=128)[:, 4 * j:4 * j + 4, :],
                         XRES[:, 4 * j:4 * j + 4, :],
                         waits=[xres_rdy[4 * j], xres_rdy[4 * j + 1],
                                xres_rdy[4 * j + 2], xres_rdy[4 * j + 3]])
                prev_store = p.cnt["x"]
                continue
            m = p.marks()
            free["SIL0"] = ("pe", m["pe"])
            y_tp, tp_tc, ln_end = emit_ln(xres_rdy, m["pe"], tmarks(m))
            xres_rdy, ffn_pe = emit_ffn(y_tp, tp_tc, m["pe"])

            m = p.marks()
            free["WA"] = ("pe", m["pe"])
            free["WB"] = ("pe", m["pe"])
            free["BV_ones"] = ("pe", m["pe"])
            y_tp, tp_tc, ln_end = emit_ln(xres_rdy, m["pe"], tmarks(m))
            ms = emit_proj(wkb, wvb, wqb, y_tp, CONST[:, 3:4])
            xres_rdy = emit_attn("b", ms, None, ln_end, m["dve"])

            m = p.marks()
            free["SIL0"] = ("pe", m["pe"])
            y_tp, tp_tc, ln_end = emit_ln(xres_rdy, m["pe"], tmarks(m))
            xres_rdy, ffn_pe = emit_ffn(y_tp, tp_tc, m["pe"])
            m = p.marks()
            free["WA"] = ("pe", m["pe"])
            free["WB"] = ("pe", m["pe"])
            last_resid = max(v for _, v in xres_rdy)

            store_of = []
            for j in range(4):
                sv = xdma(dout[s_i].rearrange("(c p) d -> p c d", p=128)[:, 4 * j:4 * j + 4, :],
                          XRES[:, 4 * j:4 * j + 4, :],
                          waits=[xres_rdy[4 * j], xres_rdy[4 * j + 1],
                                 xres_rdy[4 * j + 2], xres_rdy[4 * j + 3]],
                          key="x%d" % j)
                store_of.append(("x%d" % j, sv))
            prev_store = store_of

        for j in range(4):
            p.wait("sync", "x%d" % j, p.cnt["x%d" % j])
            p.wait("scalar", "x%d" % j, p.cnt["x%d" % j])

        with nc.Block() as block:
            def make_section(eng):
                def section(h):
                    for op in p.ops[eng]:
                        if op[0] == "w":
                            h.wait_ge(sems[op[1]], op[2])
                        else:
                            _, fn, inck, by = op
                            ins = fn(h)
                            if inck is not None:
                                ins.then_inc(sems[inck], by)
                return section
            for eng in ENGS:
                getattr(block, eng)(make_section(eng))

    return nc


_NC_CACHE = {}


def kernel(L0_attn_a_x, L0_attn_a_Wq, L0_attn_a_Wk, L0_attn_a_Wv, L0_attn_a_Wo,
           L0_attn_a_bo, L0_attn_a_denom_bias, L0_attn_b_Wq, L0_attn_b_Wk,
           L0_attn_b_Wv, L0_ffn_gate_w, L0_ffn_w_gate, L0_ffn_w_up,
           L0_ffn_w_down, L0_ffn_sw_gate, L0_ffn_sw_up, L0_ffn_sw_down,
           L0_ffn_shared_gate_w):
    x = np.asarray(L0_attn_a_x)
    tr = lambda w: np.ascontiguousarray(np.asarray(w).T)
    gw16 = np.zeros((16, D), np.float32)
    gw16[0:8] = np.asarray(L0_ffn_gate_w)
    gw16[8] = np.asarray(L0_ffn_shared_gate_w)[0]
    mask = np.triu(np.full((128, 128), -1e30, np.float32), k=1)

    common = {
        "wqa": tr(L0_attn_a_Wq), "wka": tr(L0_attn_a_Wk),
        "wva": tr(L0_attn_a_Wv), "woa": tr(L0_attn_a_Wo),
        "wqb": tr(L0_attn_b_Wq), "wkb": tr(L0_attn_b_Wk),
        "wvb": tr(L0_attn_b_Wv),
        "bo": np.ascontiguousarray(np.asarray(L0_attn_a_bo).reshape(1, D)),
        "db": np.ascontiguousarray(
            np.asarray(L0_attn_a_denom_bias).reshape(1, HEADS).astype(np.float32)),
        "gwt": tr(gw16),
        "w_gate": np.ascontiguousarray(L0_ffn_w_gate),
        "w_up": np.ascontiguousarray(L0_ffn_w_up),
        "w_down": np.ascontiguousarray(L0_ffn_w_down),
        "sw_gate": np.ascontiguousarray(L0_ffn_sw_gate),
        "sw_up": np.ascontiguousarray(L0_ffn_sw_up),
        "sw_down": np.ascontiguousarray(L0_ffn_sw_down),
        "mask": mask,
        "maskt": np.ascontiguousarray(mask.T),
        "idn": np.eye(128, dtype=ml_dtypes.bfloat16),
    }
    in_maps = []
    for c in range(NCORES):
        m = dict(common)
        m["x"] = np.ascontiguousarray(x[c * NSEQ:(c + 1) * NSEQ])
        in_maps.append(m)

    if "nc" not in _NC_CACHE:
        _NC_CACHE["nc"] = build_nc()
    nc = _NC_CACHE["nc"]
    res = run_bass_kernel_spmd(nc, in_maps, core_ids=list(range(NCORES)))
    return np.concatenate([np.asarray(r["out"], np.float32) for r in res.results],
                          axis=0)


# version 27
# speedup vs baseline: 1.1085x; 1.0107x over previous
import contextlib
import numpy as np
import ml_dtypes
import concourse.bass as bass
import concourse.mybir as mybir
from concourse.bass_utils import run_bass_kernel_spmd
from concourse.alu_op_type import AluOpType as OP

F32 = mybir.dt.float32
BF16 = mybir.dt.bfloat16
AF = mybir.ActivationFunctionType
AX = mybir.AxisListType

NCORES = 8
NSEQ = 4
S = 2048
D = 1024
NT = 16
ND = 8
HEADS = 16
HD = 64
NEXP = 9
DI = 1408
NI = 11
FMAX = float(np.finfo(np.float32).max)

ENGS = ("tensor", "scalar", "vector", "sync", "gpsimd")
SEMK = ("pe", "act", "dve", "t0", "t1", "t2", "t3", "w0", "w1", "w2", "w3", "wc", "x", "x0", "x1", "x2", "x3")

BK = 0
BV = 16384
BWA = 33024
BWB = 41216
BOF = 49408
BIG_SZ = 50688
FSLOT = [0, 11264, 22528, 33792]
BH = 45056


class Prog:
    def __init__(self):
        self.ops = {e: [] for e in ENGS}
        self.cnt = {k: 0 for k in SEMK}
        self.waited = {e: {k: 0 for k in SEMK} for e in ENGS}

    def wait(self, eng, key, val):
        if val is None or val <= 0:
            return
        if self.waited[eng][key] >= val:
            return
        self.waited[eng][key] = val
        self.ops[eng].append(("w", key, val))

    def emit(self, eng, fn, waits=(), inc=None, by=1):
        for k, v in waits:
            self.wait(eng, k, v)
        val = None
        if inc is not None:
            self.cnt[inc] += by
            val = self.cnt[inc]
        self.ops[eng].append(("i", fn, inc, by))
        return val

    def marks(self):
        return dict(self.cnt)


def ceil_div(a, b):
    return (a + b - 1) // b


def build_nc(n_seq=NSEQ, only_attn_a=False):
    nc = bass.Bass("TRN2", target_bir_lowering=False)

    dx = nc.dram_tensor("x", [n_seq, S, D], F32, kind="ExternalInput").ap()
    dout = nc.dram_tensor("out", [n_seq, S, D], F32, kind="ExternalOutput").ap()
    wqa = nc.dram_tensor("wqa", [D, D], F32, kind="ExternalInput").ap()
    wka = nc.dram_tensor("wka", [D, D], F32, kind="ExternalInput").ap()
    wva = nc.dram_tensor("wva", [D, D], F32, kind="ExternalInput").ap()
    woa = nc.dram_tensor("woa", [D, D], F32, kind="ExternalInput").ap()
    wqb = nc.dram_tensor("wqb", [D, D], F32, kind="ExternalInput").ap()
    wkb = nc.dram_tensor("wkb", [D, D], F32, kind="ExternalInput").ap()
    wvb = nc.dram_tensor("wvb", [D, D], F32, kind="ExternalInput").ap()
    dbo = nc.dram_tensor("bo", [1, D], F32, kind="ExternalInput").ap()
    ddb = nc.dram_tensor("db", [1, HEADS], F32, kind="ExternalInput").ap()
    dgw = nc.dram_tensor("gwt", [D, 16], F32, kind="ExternalInput").ap()
    dwg = nc.dram_tensor("w_gate", [8, D, DI], F32, kind="ExternalInput").ap()
    dwu = nc.dram_tensor("w_up", [8, D, DI], F32, kind="ExternalInput").ap()
    dwd = nc.dram_tensor("w_down", [8, DI, D], F32, kind="ExternalInput").ap()
    dsg = nc.dram_tensor("sw_gate", [D, DI], F32, kind="ExternalInput").ap()
    dsu = nc.dram_tensor("sw_up", [D, DI], F32, kind="ExternalInput").ap()
    dsd = nc.dram_tensor("sw_down", [DI, D], F32, kind="ExternalInput").ap()
    dmask = nc.dram_tensor("mask", [128, 128], F32, kind="ExternalInput").ap()
    dmaskt = nc.dram_tensor("maskt", [128, 128], F32, kind="ExternalInput").ap()
    didn = nc.dram_tensor("idn", [128, 128], BF16, kind="ExternalInput").ap()

    p = Prog()

    with contextlib.ExitStack() as st:
        sb = lambda name, shape, dt: st.enter_context(nc.sbuf_tensor(name, shape, dt))
        psumt = lambda name, shape: st.enter_context(nc.psum_tensor(name, shape, F32))

        XRES = sb("XRES", [128, NT, D], F32)
        Y = sb("Y", [128, ND, S], BF16)
        BIG = sb("BIG", [128, BIG_SZ], BF16)
        PT_ = sb("PT", [128, 2, 512], BF16)
        OUTTM = sb("OUTTM", [128, D], BF16)
        SIL = sb("SIL", [128, 512], F32)
        LNS = sb("LNS", [128, 4, NT], F32)
        AT = sb("AT", [128, 64], F32)
        NEGM = sb("NEGM", [128, 16], F32)
        U16 = sb("U16", [128, 16], F32)
        T16 = sb("T16", [128, 16], F32)
        SM = sb("SM", [128, 8], F32)
        SM2 = sb("SM2", [128, 8], F32)
        RT = sb("RT", [128, 40], F32)
        CB = sb("CB", [128, NT, 8], F32)
        CBSG = sb("CBSG", [128, NT], F32)
        MASKSB = sb("MASKSB", [128, 128], F32)
        MASKT = sb("MASKT", [128, 128], F32)
        ONESB = sb("ONESB", [128, 1], BF16)
        BOROW = sb("BOROW", [1, D], BF16)
        DBROW = sb("DBROW", [1, HEADS], F32)
        EDBROW = sb("EDBROW", [1, HEADS], F32)
        EDBB = sb("EDBB", [128, HEADS], F32)
        GWT = sb("GWT", [128, ND, 16], BF16)
        ONES = sb("ONES", [1, 128], BF16)
        ONESF = sb("ONESF", [1, 128], F32)
        CONST = sb("CONST", [128, 8], F32)

        PA = psumt("PA", [128, 1024])
        PBp = psumt("PBp", [128, 1024])
        PCD = psumt("PCD", [128, 1024])
        PC = PCD[:, 0:512]
        PD = PCD[:, 512:1024]
        PE2 = psumt("PE2", [128, 512])
        PF = psumt("PF", [128, 512])

        sems = {k: st.enter_context(nc.semaphore("s_" + k)) for k in SEMK}

        QBANKS = [PA[:, 0:512], PA[:, 512:1024], PBp[:, 0:512], PBp[:, 512:1024],
                  PC[:, 0:512], PD[:, 0:512], PE2[:, 0:512], PF[:, 0:512]]
        SBANKS = [PC, PD]

        big = lambda off, ln: BIG[:, off:off + ln]

        def mm(out_ap, lhsT, rhs, start, stop, waits=(), inc=None):
            return p.emit("tensor",
                          lambda h, o=out_ap, l=lhsT, r=rhs, s1=start, s2=stop:
                          h.matmul(o, l, r, start=s1, stop=s2),
                          waits=waits, inc=inc)

        def act(fn, waits=(), inc="act"):
            w = list(waits) + [("act", p.cnt["act"])]
            return p.emit("scalar", fn, waits=w, inc=inc)

        def dve(fn, waits=(), inc="dve"):
            w = list(waits) + [("dve", p.cnt["dve"])]
            return p.emit("vector", fn, waits=w, inc=inc)

        def tp(dst, src, key, waits=()):
            return p.emit("sync",
                          lambda h, d=dst, s=src: h.dma_start_transpose(d, s),
                          waits=waits, inc=key, by=16)

        def wload(dst, src, key, waits=()):
            return p.emit("gpsimd",
                          lambda h, d=dst, s=src: h.dma_start(out=d, in_=s),
                          waits=waits, inc=key, by=16)

        def xdma(dst, src, waits=(), key="x"):
            w = list(waits) + [("act", p.cnt["act"])]
            return p.emit("scalar",
                          lambda h, d=dst, s=src: h.dma_start(out=d, in_=s),
                          waits=w, inc=key, by=16)

        wload(MASKSB[:, 0:64].bitcast(BF16), dmask, "wc")
        wload(MASKT[:, 0:64].bitcast(BF16), dmaskt, "wc")
        wload(BOROW[:], dbo, "wc")
        wload(DBROW[:], ddb, "wc")
        wload(GWT[:], dgw.rearrange("(dc p) e -> p dc e", p=128), "wc")
        w_const = p.cnt["wc"]
        dve(lambda h: h.memset(ONES[:], 1.0))
        dve(lambda h: h.memset(ONESF[:], 1.0))
        dve(lambda h: h.memset(CONST[:, 0:1], 1e-5))
        dve(lambda h: h.memset(CONST[:, 1:2], -0.5))
        dve(lambda h: h.memset(CONST[:, 2:3], 0.125))
        dve(lambda h: h.memset(CONST[:, 3:4], 1.0 / 32.0))
        dve(lambda h: h.memset(CONST[:, 4:5], -30.0))
        dve(lambda h: h.memset(CONST[:, 5:6], -1.0))
        dve(lambda h: h.memset(ONESB[:], 1.0))
        ones_ready = p.cnt["dve"]
        edb_v = act(lambda h: h.activation(out=EDBROW[:], in_=DBROW[:], func=AF.Exp),
                    waits=[("wc", w_const)])
        mm(PC[:, 0:16], ONESF[:], EDBROW[:], True, True,
           waits=[("dve", ones_ready), ("act", edb_v)], inc="pe")
        pe_edb = p.cnt["pe"]
        act(lambda h: h.copy(out=EDBB[:], in_=PC[:, 0:16]), waits=[("pe", pe_edb)])
        edbb_ready = p.cnt["act"]

        state = {
            "xres_dve": 0,
            "x_load": 0,
            "pc_free": ("act", edbb_ready),
        }

        free = {
            "PA0": ("pe", 0), "PA1": ("pe", 0), "PB0": ("pe", 0), "PB1": ("pe", 0),
            "PC": ("act", edbb_ready), "PD": ("pe", 0), "PE2": ("pe", 0), "PF": ("pe", 0),
            "PTS0": ("pe", 0), "PTS1": ("pe", 0), "PTS2": ("pe", 0), "PTS3": ("pe", 0),
            "OUTTM": ("pe", 0), "SIL0": ("pe", 0), "SIL1": ("pe", 0),
            "Y": ("pe", 0), "BIG": ("pe", 0), "XRES_w": ("x", 0),
            "WA": ("pe", 0), "WB": ("pe", 0), "OF": ("pe", 0),
            "SLOT0": ("pe", 0), "SLOT1": ("pe", 0), "SLOT2": ("pe", 0), "SLOT3": ("pe", 0),
            "H": ("pe", 0),
        }

        def emit_ln(xres_rdy, y_free_pe, m_tp):
            yflat = Y[:].rearrange("p a b -> p (a b)")
            sq_scr = yflat[:, 15 * 1024:16 * 1024]
            ysc = [OUTTM[:], PT_[:].rearrange("p a b -> p (a b)")]
            for tc in range(NT):
                act(lambda h, tc=tc, o=sq_scr: h.activation(
                    out=o, in_=XRES[:, tc, :], func=AF.Square,
                    accum_out=LNS[:, 1, tc:tc + 1]),
                    waits=[xres_rdy[tc], ("pe", y_free_pe)])
                dve(lambda h, tc=tc, o=yflat[:, 14 * 1024:15 * 1024]: h.tensor_scalar(
                    out=o, in0=XRES[:, tc, :], scalar1=0.0, scalar2=0.0,
                    op0=OP.add, op1=OP.add, accum_out=LNS[:, 0, tc:tc + 1]),
                    waits=[xres_rdy[tc], ("pe", y_free_pe)])
            sumsq_done = p.cnt["act"]
            dve(lambda h: h.tensor_scalar(out=LNS[:, 2, :], in0=LNS[:, 0, :],
                                          scalar1=-1.0 / D, scalar2=None, op0=OP.mult))
            dve(lambda h: h.tensor_tensor(out=LNS[:, 0, :], in0=LNS[:, 2, :],
                                          in1=LNS[:, 2, :], op=OP.mult))
            var_v = dve(lambda h: h.scalar_tensor_tensor(
                out=LNS[:, 1, :], in0=LNS[:, 1, :], scalar=1.0 / D,
                in1=LNS[:, 0, :], op0=OP.mult, op1=OP.subtract),
                waits=[("act", sumsq_done)])
            lnv = act(lambda h: h.activation(out=LNS[:, 0, :], in_=LNS[:, 1, :],
                                             func=AF.Ln, bias=CONST[:, 0:1]),
                      waits=[("dve", var_v), ("dve", ones_ready)])
            rstd_v = act(lambda h: h.activation(out=LNS[:, 3, :], in_=LNS[:, 0, :],
                                                func=AF.Exp, scale=CONST[:, 1:2]))
            y_tp = []
            tp_of_tc = []
            IDN = big(50432, 128)
            idn_ld = wload(IDN, didn, "wc",
                           waits=[("pe", y_free_pe)] + list(m_tp))
            tp_bank = lambda i: (PA if i < 16 else PBp)[:, (i % 16) * 64:(i % 16) * 64 + 64].bitcast(BF16)
            evac_hist = []
            for tc in range(NT):
                scr = ysc[tc % 2]
                w_tp = [tp_of_tc[tc - 2]] if tc >= 2 else m_tp
                nv = dve(lambda h, tc=tc, o=scr: h.tensor_scalar(
                    out=o, in0=XRES[:, tc, :],
                    scalar1=LNS[:, 2, tc:tc + 1], scalar2=LNS[:, 3, tc:tc + 1],
                    op0=OP.add, op1=OP.mult),
                    waits=[("act", rstd_v), ("pe", y_free_pe)] + list(w_tp))
                last_tp = None
                for dc in range(ND):
                    ti = (tc % 2) * 8 + dc
                    wv = [("dve", nv), ("wc", idn_ld)]
                    if tc < 2:
                        wv += list(m_tp)
                    if len(evac_hist) >= 2:
                        wv.append(("act", evac_hist[-2]))
                    last_tp = p.emit("tensor",
                                     lambda h, o=tp_bank(ti),
                                     i_=scr[:, dc * 128:(dc + 1) * 128], idn=IDN:
                                     h.transpose(o, i_, idn),
                                     waits=wv, inc="pe")
                ev = None
                for dc in range(ND):
                    ti = (tc % 2) * 8 + dc
                    ev = act(lambda h, tc=tc, dc=dc, i_=tp_bank(ti): h.copy(
                        out=Y[:, dc, tc * 128:(tc + 1) * 128], in_=i_),
                        waits=[("pe", last_tp)])
                evac_hist.append(ev)
                tp_of_tc.append(("act", ev))
            for tt in range(4):
                y_tp.append([tp_of_tc[tt * 4 + i] for i in range(4)])
            for nm in ("PA0", "PA1", "PB0", "PB1"):
                free[nm] = ("act", p.cnt["act"])
            ln_end = [("act", p.cnt["act"])]
            return y_tp, tp_of_tc, ln_end

        def emit_proj(wk_d, wv_d, wq_d, y_tp, qscale_ap):
            wa_ld = wload(big(BWA, 8192).rearrange("p (dc o) -> p dc o", o=D),
                          wk_d.rearrange("(dc p) o -> p dc o", p=128), "w0",
                          waits=[free["WA"]])
            wb_ld = wload(big(BWB, 8192).rearrange("p (dc o) -> p dc o", o=D),
                          wv_d.rearrange("(dc p) o -> p dc o", p=128), "w1",
                          waits=[free["WB"]])
            dve(lambda h: h.memset(
                bass.AP(BIG, BV + 64, [[BIG_SZ, 128], [1040, 16], [65, 16]]), 1.0),
                waits=[free["BV_ones"]] if "BV_ones" in free else [])
            ones_v = p.cnt["dve"]
            kbank_free = [("pe", 0)] * 4
            kv = 0
            for tt in range(4):
                for oc in range(ND):
                    bidx = (tt * ND + oc) % 4
                    bank = [PC, PD, PE2, PF][bidx]
                    waits = [("w0", wa_ld), kbank_free[bidx],
                             free[["PC", "PD", "PE2", "PF"][bidx]]] + list(y_tp[tt])
                    for dc in range(ND):
                        g = mm(bank[:, 0:512],
                               big(BWA, 8192)[:, dc * D + oc * 128: dc * D + (oc + 1) * 128],
                               Y[:, dc, tt * 512:(tt + 1) * 512],
                               dc == 0, dc == ND - 1,
                               waits=waits if dc == 0 else (),
                               inc="pe" if dc == ND - 1 else None)
                    kv = act(lambda h, oc=oc, tt=tt, b=bank: h.copy(
                        out=big(BK, 16384).rearrange("p (c t) -> p c t", c=ND)[:, oc, tt * 512:(tt + 1) * 512],
                        in_=b[:, 0:512]), waits=[("pe", g)])
                    kbank_free[bidx] = ("act", kv)
            k_done = kv
            kproj_pe = p.cnt["pe"]
            vfree = [free["PA0"], free["PA1"], free["PB0"], free["PB1"]]
            vv = 0
            for tc in range(NT):
                bank = PA if tc % 2 == 0 else PBp
                bw = [vfree[0], vfree[1]] if tc % 2 == 0 else [vfree[2], vfree[3]]
                gl = None
                for half in range(2):
                    for dc in range(ND):
                        gl = mm(bank[:, half * 512:(half + 1) * 512],
                                Y[:, dc, tc * 128:(tc + 1) * 128],
                                big(BWB, 8192)[:, dc * D + half * 512: dc * D + (half + 1) * 512],
                                dc == 0, dc == ND - 1,
                                waits=([("w1", wb_ld), bw[half]] if dc == 0 else ()),
                                inc="pe" if dc == ND - 1 else None)
                vv = act(lambda h, tc=tc, b=bank: h.copy(
                    out=bass.AP(BIG, BV + tc * 1040, [[BIG_SZ, 128], [65, 16], [1, 64]]),
                    in_=b[:]), waits=[("pe", gl)])
                if tc % 2 == 0:
                    vfree[0] = vfree[1] = ("act", vv)
                else:
                    vfree[2] = vfree[3] = ("act", vv)
            v_done = vv
            vproj_pe = p.cnt["pe"]
            wq_ld = wload(big(BWA, 8192).rearrange("p (dc o) -> p dc o", o=D),
                          wq_d.rearrange("(dc p) o -> p dc o", p=128), "w0",
                          waits=[("pe", kproj_pe)])
            evac_of_bank = [("act", 0)] * 8
            for tt in range(4):
                last_g = None
                for oc in range(ND):
                    bank = QBANKS[oc]
                    waits = [("w0", wq_ld), evac_of_bank[oc]]
                    if tt == 0:
                        waits.append(("act", v_done))
                        waits.append(("act", k_done))
                    for dc in range(ND):
                        last_g = mm(bank, big(BWA, 8192)[:, dc * D + oc * 128: dc * D + (oc + 1) * 128],
                                    Y[:, dc, tt * 512:(tt + 1) * 512],
                                    dc == 0, dc == ND - 1,
                                    waits=waits if dc == 0 else (),
                                    inc="pe" if (dc == ND - 1 and oc == ND - 1) else None)
                for oc in range(ND):
                    ev = act(lambda h, oc=oc, tt=tt, b=QBANKS[oc]: h.activation(
                        out=Y[:, oc, tt * 512:(tt + 1) * 512], in_=b,
                        func=AF.Copy, scale=qscale_ap),
                        waits=[("pe", last_g), ("dve", ones_ready)])
                    evac_of_bank[oc] = ("act", ev)
            q_done = p.cnt["act"]
            for oc, nm in enumerate(["PA0", "PA1", "PB0", "PB1",
                                     "PC", "PD", "PE2", "PF"]):
                free[nm] = evac_of_bank[oc]
            return dict(k_done=k_done, v_done=v_done, q_done=q_done,
                        vproj_pe=vproj_pe, kproj_pe=kproj_pe,
                        qproj_pe=p.cnt["pe"], ones_v=ones_v)

        PTF = PT_[:].rearrange("p a b -> p (a b)")

        def pt_slot(c, par, lo, ln):
            if par == 0:
                return PTF[:, c * 512 + lo: c * 512 + lo + ln]
            return SIL[:, (c * 512 + lo) // 2: (c * 512 + lo + ln) // 2].bitcast(BF16)

        def emit_attn(kind, ms, wo_d, ln_end_tp, m_dve=0):
            kview = big(BK, 16384).rearrange("p (c t) -> p c t", c=ND)
            v65 = lambda kc, h_i: BIG[:, BV + kc * 1040 + h_i * 65: BV + kc * 1040 + h_i * 65 + 65]
            vb_half = lambda kc, half: bass.AP(
                BIG, BV + kc * 1040 + half * 8 * 65,
                [[BIG_SZ, 128], [65, 8], [1, 64]])
            vb_one = lambda kc: BIG[:, BV + kc * 1040 + 64: BV + kc * 1040 + 65]
            qk_w = [("act", ms["q_done"]), ("act", ms["k_done"])]
            resid = []
            if kind == "a":
                wo_ld = wload(big(BWB, 8192).rearrange("p (dc o) -> p dc o", o=D),
                              wo_d.rearrange("(dc p) o -> p dc o", p=128), "w1",
                              waits=[("pe", ms["vproj_pe"])])
                sfree = {(0, 0): [free["PC"]], (0, 1): [free["PA0"]],
                         (1, 0): [free["PD"]], (1, 1): [free["PA1"]]}
                hfree = [free["PB0"], free["PB1"]]
                ptfree = {(0, 0): [free["PTS0"], free["PTS1"]] + list(ln_end_tp),
                          (1, 0): [free["PTS2"], free["PTS3"]] + list(ln_end_tp),
                          (0, 1): [("dve", m_dve)],
                          (1, 1): [("dve", m_dve)]}
                accfree = [[free["PE2"]], [free["PF"]]]
                pafree = [[free["PA0"]], [free["PA1"]]]
                outtm_free = list(ln_end_tp)
                slot_i = 0
                pend_op = None

                def emit_oproj(qc_, of_tp_):
                    zg = None
                    for half in range(2):
                        for dc in range(ND):
                            mm(PA[:, half * 512:(half + 1) * 512],
                               big(BOF, 1024)[:, dc * 128:(dc + 1) * 128],
                               big(BWB, 8192)[:, dc * D + half * 512: dc * D + (half + 1) * 512],
                               dc == 0, False,
                               waits=(([("t2", of_tp_), ("w1", wo_ld),
                                        ("dve", ones_ready), ("wc", w_const)]
                                       + pafree[half]
                                       + sfree[(half, 1)])
                                      if dc == 0 else ()))
                        zg = mm(PA[:, half * 512:(half + 1) * 512], ONES[:],
                                BOROW[:, half * 512:(half + 1) * 512],
                                False, True, inc="pe")
                    free["OF"] = ("pe", p.cnt["pe"])
                    rv = dve(lambda h, qc_=qc_: h.tensor_tensor(
                        out=XRES[:, qc_, :], in0=XRES[:, qc_, :], in1=PA[:],
                        op=OP.add), waits=[("pe", zg)])
                    pafree[0] = pafree[1] = [("dve", rv)]
                    sfree[(0, 1)] = [("dve", rv)]
                    sfree[(1, 1)] = [("dve", rv)]
                    resid.append(("dve", rv))

                for qc in range(NT):
                    klen = (qc + 1) * 128
                    nkt = ceil_div(klen, 512)
                    widths = [min(512, klen - 512 * j) for j in range(nkt)]
                    nck = qc + 1
                    for hgp in range(2):
                        hgs = (2 * hgp, 2 * hgp + 1)
                        ph1_banks = [PBp[:, 0:512], PBp[:, 512:1024], PC, PD]
                        ph1_pairs = [PBp[:], PCD[:]]
                        ph1_free = [hfree[0], hfree[1],
                                    sfree[(0, 0)][0], sfree[(1, 0)][0]]
                        bctr = 0
                        for hg in hgs:
                            for j4 in range(4):
                                h_i = hg * 4 + j4
                                pbase, oc_h = 64 * (h_i % 2), h_i // 2
                                q_ap = Y[pbase:pbase + 64, oc_h, qc * 128:(qc + 1) * 128]
                                if nkt <= 2:
                                    pr = h_i % 2
                                    pair = ph1_pairs[pr]
                                    g = None
                                    for kt in range(nkt):
                                        w_ = widths[kt]
                                        g = mm(pair[:, kt * 512:kt * 512 + w_],
                                               q_ap,
                                               kview[pbase:pbase + 64, oc_h, kt * 512:kt * 512 + w_],
                                               True, kt != nkt - 1,
                                               waits=qk_w + [ph1_free[pr * 2],
                                                             ph1_free[pr * 2 + 1]]
                                                     if kt == 0 else qk_w,
                                               inc="pe" if kt == nkt - 1 else None)
                                    gm = mm(pair[:, klen - 128:klen],
                                            big(50432, 128),
                                            MASKSB[:, 0:64].bitcast(BF16),
                                            False, True,
                                            waits=[("pe", g), ("wc", w_const)],
                                            inc="pe")
                                    mx = dve(lambda h, h_i=h_i, b=pair, w_=klen:
                                             h.tensor_reduce(
                                                 out=NEGM[:, h_i:h_i + 1],
                                                 in_=b[:, 0:w_], axis=AX.X,
                                                 op=OP.max, negate=True),
                                             waits=[("pe", gm)])
                                    ph1_free[pr * 2] = ("dve", mx)
                                    ph1_free[pr * 2 + 1] = ("dve", mx)
                                    continue
                                for kt in range(nkt):
                                    w_ = widths[kt]
                                    bsel = bctr % 4
                                    bctr += 1
                                    bank = ph1_banks[bsel]
                                    g = mm(bank[:, 0:w_], q_ap,
                                           kview[pbase:pbase + 64, oc_h, kt * 512:kt * 512 + w_],
                                           True, kt != nkt - 1,
                                           waits=qk_w + [ph1_free[bsel]], inc="pe")
                                    wl = [("pe", g)]
                                    if kt == nkt - 1:
                                        gm = mm(bank[:, w_ - 128:w_],
                                                big(50432, 128),
                                                MASKSB[:, 0:64].bitcast(BF16),
                                                False, True,
                                                waits=[("pe", g), ("wc", w_const)],
                                                inc="pe")
                                        wl = [("pe", gm)]
                                    mx = dve(lambda h, kt=kt, h_i=h_i, b=bank, w_=w_:
                                             h.tensor_reduce(
                                                 out=AT[:, h_i * 4 + kt:h_i * 4 + kt + 1],
                                                 in_=b[:, 0:w_], axis=AX.X, op=OP.max),
                                             waits=wl)
                                    ph1_free[bsel] = ("dve", mx)
                                dve(lambda h, h_i=h_i, n=nkt: h.tensor_reduce(
                                    out=NEGM[:, h_i:h_i + 1],
                                    in_=AT[:, h_i * 4:h_i * 4 + n],
                                    axis=AX.X, op=OP.max, negate=True))
                        hfree[0], hfree[1] = ph1_free[0], ph1_free[1]
                        sfree[(0, 0)] = [ph1_free[2]]
                        sfree[(1, 0)] = [ph1_free[3]]
                        negm8 = p.cnt["dve"]
                        uu = act(lambda h, hg=hgs[0]: h.activation(
                            out=U16[:, hg * 4:hg * 4 + 8],
                            in_=NEGM[:, hg * 4:hg * 4 + 8], func=AF.Exp,
                            scale=CONST[:, 5:6], bias=CONST[:, 4:5]),
                            waits=[("dve", negm8), ("dve", ones_ready)])
                        t16 = dve(lambda h, hg=hgs[0]: h.tensor_tensor(
                            out=T16[:, hg * 4:hg * 4 + 8],
                            in0=U16[:, hg * 4:hg * 4 + 8],
                            in1=EDBB[:, hg * 4:hg * 4 + 8], op=OP.mult),
                            waits=[("act", uu), ("act", edbb_ready)])
                        if hgp == 0 and pend_op is not None:
                            emit_oproj(*pend_op)
                            pend_op = None
                        jpos = {0: 0, 2: 128, 1: 256, 3: 384}
                        abank_of = {0: PE2, 1: PF}
                        pv_last = {0: None, 1: None}
                        pend_pv = None

                        def emit_pvs(job):
                            kcp, parp, exd = job
                            for c in (0, 1):
                                wpv = [("act", exd[c])]
                                if kcp == 0:
                                    wpv += [("act", ms["v_done"]),
                                            ("dve", ms["ones_v"])] + accfree[c]
                                for j in range(4):
                                    h_i = hgs[c] * 4 + j
                                    lhs = pt_slot(c, parp, jpos[j], 128)
                                    pv_last[c] = mm(
                                        abank_of[c][:, j * 128:j * 128 + 65], lhs,
                                        v65(kcp, h_i),
                                        kcp == 0 and j == 0,
                                        kcp == nck - 1 and j == 3,
                                        waits=wpv if j == 0 else (),
                                        inc="pe" if (kcp == nck - 1 and j == 3) else None)
                                ptfree[(c, parp)] = [("pe", pv_last[c])]

                        for kc in range(nck):
                            par = kc % 2
                            bank_of = {0: PC if par == 0 else PA[:, 0:512],
                                       1: PD if par == 0 else PA[:, 512:1024]}
                            mm_done = {}
                            for c in (0, 1):
                                bank = bank_of[c]
                                g0 = None
                                for j in (0, 2):
                                    h_i = hgs[c] * 4 + j
                                    oc_h = h_i // 2
                                    g0 = mm(bank[:, jpos[j]:jpos[j] + 128],
                                            kview[0:64, oc_h, kc * 128:(kc + 1) * 128],
                                            Y[0:64, oc_h, qc * 128:(qc + 1) * 128],
                                            j == 0, False,
                                            waits=list(sfree[(c, par)]) if j == 0 else (),
                                            inc="pe" if j == 2 else None)
                                g1 = None
                                for j in (1, 3):
                                    h_i = hgs[c] * 4 + j
                                    oc_h = h_i // 2
                                    g1 = mm(bank[:, jpos[j]:jpos[j] + 128],
                                            kview[64:128, oc_h, kc * 128:(kc + 1) * 128],
                                            Y[64:128, oc_h, qc * 128:(qc + 1) * 128],
                                            False, j == 3 and kc != qc,
                                            waits=[("pe", g0)] if j == 1 else (),
                                            inc="pe" if j == 3 else None)
                                mm_done[c] = ("pe", g1)
                            if kc == qc:
                                for c in (0, 1):
                                    gd = None
                                    for j in range(4):
                                        gd = mm(bank_of[c][:, jpos[j]:jpos[j] + 128],
                                                big(50432, 128),
                                                MASKT[:, 0:64].bitcast(BF16),
                                                False, j == 3,
                                                waits=[mm_done[c], ("wc", w_const)]
                                                      if j == 0 else (),
                                                inc="pe" if j == 3 else None)
                                    mm_done[c] = ("pe", gd)
                            src = PCD[:] if par == 0 else PA[:]
                            dst = (PTF[:, 0:1024] if par == 0
                                   else SIL[:, 0:512].bitcast(BF16))
                            ex = act(lambda h, b=src, o=dst: h.activation(
                                out=o, in_=b, func=AF.Exp, bias=CONST[:, 4:5]),
                                waits=[mm_done[0], mm_done[1], ("dve", ones_ready)]
                                      + list(ptfree[(0, par)]) + list(ptfree[(1, par)]))
                            sfree[(0, par)] = [("act", ex)]
                            sfree[(1, par)] = [("act", ex)]
                            exd = {0: ex, 1: ex}
                            if pend_pv is not None:
                                emit_pvs(pend_pv)
                            pend_pv = (kc, par, exd)
                        emit_pvs(pend_pv)
                        ot = None
                        for c in (0, 1):
                            for j in range(4):
                                h_i = hgs[c] * 4 + j
                                dve(lambda h, h_i=h_i, j=j, c=c, a=abank_of[c]:
                                    h.tensor_tensor(
                                        out=SM[:, c * 4 + j:c * 4 + j + 1],
                                        in0=T16[:, h_i:h_i + 1],
                                        in1=a[:, j * 128 + 64:j * 128 + 65],
                                        op=OP.add),
                                    waits=[("dve", t16), ("pe", pv_last[c])])
                        rec = dve(lambda h: h.reciprocal(out=SM2[:], in_=SM[:]))
                        for c in (0, 1):
                            for j in range(4):
                                h_i = hgs[c] * 4 + j
                                ot = act(lambda h, h_i=h_i, j=j, c=c, a=abank_of[c]:
                                         h.activation(
                                             out=OUTTM[:, 64 * h_i:64 * h_i + 64],
                                             in_=a[:, j * 128:j * 128 + 64],
                                             func=AF.Copy,
                                             scale=SM2[:, c * 4 + j:c * 4 + j + 1]),
                                         waits=[("dve", rec)] + list(outtm_free))
                            accfree[c] = [("act", ot)]
                    heads_done = p.cnt["act"]
                    for dc in range(ND):
                        tp(big(BOF, 1024)[:, dc * 128:(dc + 1) * 128],
                           OUTTM[:, dc * 128:(dc + 1) * 128], "t2",
                           waits=[("act", heads_done), free["OF"]])
                    of_tp = p.cnt["t2"]
                    outtm_free = [("t2", of_tp)]
                    pend_op = (qc, of_tp)
                if pend_op is not None:
                    emit_oproj(*pend_op)
                free["PC"] = sfree[(0, 0)][0]
                free["PD"] = sfree[(1, 0)][0]
                free["PB0"] = hfree[0]
                free["PB1"] = hfree[1]
                free["PE2"] = accfree[0][0]
                free["PF"] = accfree[1][0]
                free["PA0"] = pafree[0][0]
                free["PA1"] = pafree[1][0]
                free["PTS0"] = free["PTS1"] = ptfree[(0, 0)][0]
                free["PTS2"] = free["PTS3"] = ptfree[(1, 0)][0]
            else:
                SB4 = [PC[:, 0:128], PD[:, 0:128],
                       PBp[:, 0:128], PBp[:, 512:640]]
                sfree = [free["PC"], free["PD"], free["PB0"], free["PB1"]]
                ptfree = [[free["PTS%d" % (i % 4)]] + list(ln_end_tp)
                          if i < 4 else [("dve", m_dve)] for i in range(8)]
                pffree = [free["PF"]]
                pafree = [free["PA0"], free["PA1"]]
                for qc in range(NT):
                    nck = qc + 1
                    pv_g = None
                    for kc in range(nck):
                        bank = SB4[kc % 4]
                        g = None
                        for dc in range(ND):
                            g = mm(bank,
                                   kview[:, dc, kc * 128:(kc + 1) * 128],
                                   Y[:, dc, qc * 128:(qc + 1) * 128],
                                   dc == 0, dc == ND - 1 and kc != qc,
                                   waits=([sfree[kc % 4]] + qk_w if dc == 0 else ()),
                                   inc="pe" if (dc == ND - 1 and kc != qc) else None)
                        lastw = ("pe", g)
                        if kc == qc:
                            g = mm(bank, big(50432, 128),
                                   MASKT[:, 0:64].bitcast(BF16),
                                   False, True, waits=[("wc", w_const)],
                                   inc="pe")
                            lastw = ("pe", g)
                        slot = kc % 8
                        po = pt_slot(slot // 4, slot % 4 // 2, (slot % 2) * 256, 128) \
                            if False else (
                            PTF[:, (slot % 4) * 128:(slot % 4) * 128 + 128] if slot < 4
                            else SIL[:, (slot - 4) * 64:(slot - 4) * 64 + 64].bitcast(BF16))
                        ex = act(lambda h, b=bank, o=po: h.activation(
                            out=o, in_=b,
                            func=AF.Exp, bias=CONST[:, 4:5]),
                            waits=[lastw, ("dve", ones_ready)] + list(ptfree[slot]))
                        sfree[kc % 4] = ("act", ex)
                        wpv = [("act", ex)]
                        if kc == 0:
                            wpv += [pafree[0], pafree[1],
                                    ("act", ms["v_done"])] + list(pffree)
                        lhs = po
                        for half in range(2):
                            mm(PA[:, half * 512:(half + 1) * 512], lhs,
                               vb_half(kc, half),
                               kc == 0, kc == nck - 1,
                               waits=wpv if half == 0 else ())
                        pv_g = mm(PF[:, 0:1], lhs, ONESB[:],
                                  kc == 0, kc == nck - 1, inc="pe")
                        ptfree[slot] = [("pe", pv_g)]
                    rec = dve(lambda h: h.reciprocal(out=SM[:, 3:4], in_=PF[:, 0:1]),
                              waits=[("pe", pv_g)])
                    pffree = [("dve", rec)]
                    rv = dve(lambda h, qc=qc: h.scalar_tensor_tensor(
                        out=XRES[:, qc, :], in0=PA[:],
                        scalar=SM[:, 3:4], in1=XRES[:, qc, :],
                        op0=OP.mult, op1=OP.add))
                    pafree = [("dve", rv), ("dve", rv)]
                    resid.append(("dve", rv))
                free["PC"] = sfree[0]
                free["PD"] = sfree[1]
                free["PB0"] = sfree[2]
                free["PB1"] = sfree[3]
                free["PA0"] = pafree[0]
                free["PA1"] = pafree[1]
                free["PF"] = pffree[0]
                for i in range(4):
                    free["PTS%d" % i] = ptfree[i][0]
                free["SILB"] = ptfree[4][0]
            return resid

        def emit_ffn(y_tp, tp_tc, slot_wait_pe):
            kexp = lambda e, j: (3 * e + j) % 4
            slot_ld = {}
            srcs = lambda e: [(dwg[e] if e < 8 else dsg),
                              (dwu[e] if e < 8 else dsu),
                              (dwd[e] if e < 8 else dsd)]
            for j in range(3):
                s_i = kexp(0, j)
                rear = "(hc p) i -> p hc i" if j < 2 else "(ic p) o -> p ic o"
                slot_ld[s_i] = wload(
                    big(FSLOT[s_i], 11264).rearrange("p (a b) -> p a b", a=(ND if j < 2 else NI)),
                    srcs(0)[j].rearrange(rear, p=128), "w%d" % s_i,
                    waits=[("pe", slot_wait_pe)])
            slot_free = {i: ("pe", slot_wait_pe) for i in range(4)}

            pc_free = free["PC"]
            for tc in range(NT):
                g = None
                for dc in range(ND):
                    g = mm(PC[:, 0:16], Y[:, dc, tc * 128:(tc + 1) * 128],
                           GWT[:, dc, :], dc == 0, dc == ND - 1,
                           waits=([("wc", w_const), tp_tc[tc], pc_free]
                                  if dc == 0 else ()),
                           inc="pe" if dc == ND - 1 else None)
                nm = dve(lambda h: h.tensor_reduce(
                    out=RT[:, 32:33], in_=PC[:, 0:8], axis=AX.X, op=OP.max,
                    negate=True), waits=[("pe", g)])
                ex = act(lambda h: h.activation(out=RT[:, 0:8], in_=PC[:, 0:8],
                                                func=AF.Exp, bias=RT[:, 32:33]),
                         waits=[("dve", nm)])
                sg = act(lambda h, tc=tc: h.activation(
                    out=CBSG[:, tc:tc + 1], in_=PC[:, 8:9], func=AF.Sigmoid))
                pc_free = ("act", sg)
                dve(lambda h: h.tensor_reduce(out=RT[:, 33:34], in_=RT[:, 0:8],
                                              axis=AX.X, op=OP.max),
                    waits=[("act", ex)])
                dve(lambda h: h.tensor_scalar(out=RT[:, 8:16], in0=RT[:, 0:8],
                                              scalar1=RT[:, 33:34], scalar2=None,
                                              op0=OP.is_ge))
                dve(lambda h: h.scalar_tensor_tensor(
                    out=RT[:, 16:24], in0=RT[:, 8:16], scalar=-1e30,
                    in1=RT[:, 0:8], op0=OP.mult, op1=OP.add))
                dve(lambda h: h.tensor_reduce(out=RT[:, 34:35], in_=RT[:, 16:24],
                                              axis=AX.X, op=OP.max))
                dve(lambda h: h.tensor_scalar(out=RT[:, 24:32], in0=RT[:, 16:24],
                                              scalar1=RT[:, 34:35], scalar2=None,
                                              op0=OP.is_ge))
                dve(lambda h: h.tensor_tensor(out=RT[:, 8:16], in0=RT[:, 8:16],
                                              in1=RT[:, 24:32], op=OP.add))
                dve(lambda h: h.tensor_tensor(out=RT[:, 33:34], in0=RT[:, 33:34],
                                              in1=RT[:, 34:35], op=OP.add))
                dve(lambda h: h.reciprocal(out=RT[:, 35:36], in_=RT[:, 33:34]))
                dve(lambda h, tc=tc: h.scalar_tensor_tensor(
                    out=CB[:, tc, :], in0=RT[:, 0:8], scalar=RT[:, 35:36],
                    in1=RT[:, 8:16], op0=OP.mult, op1=OP.mult))
            router_dve = p.cnt["dve"]
            router_act = p.cnt["act"]
            free["PC"] = pc_free

            gb = [PC, PE2]
            ub = [PD, PF]
            gfree = [free["PC"], free["PE2"]]
            ufree = [free["PD"], free["PF"]]
            dfree = [free["PA0"], free["PB0"]]
            silfree = [free["SIL0"]]
            hfree = ("pe", slot_wait_pe)
            resid = [None] * NT
            for e in range(NEXP):
                wg_s, wu_s, wd_s = (kexp(e, 0), kexp(e, 1), kexp(e, 2))
                wg_last = wu_last = wd_last = None
                for tt in range(4):
                    h_dve = []
                    for ic in range(NI):
                        par = ic % 2
                        g = None
                        for hc in range(ND):
                            g = mm(gb[par][:, 0:512],
                                   big(FSLOT[wg_s], 11264)[:, hc * DI + ic * 128: hc * DI + (ic + 1) * 128],
                                   Y[:, hc, tt * 512:(tt + 1) * 512],
                                   hc == 0, hc == ND - 1,
                                   waits=(([("w%d" % wg_s, slot_ld[wg_s]),
                                           gfree[par], hfree] + list(y_tp[tt])) if hc == 0 else ()),
                                   inc="pe" if hc == ND - 1 else None)
                        wg_last = g
                        u = None
                        for hc in range(ND):
                            u = mm(ub[par][:, 0:512],
                                   big(FSLOT[wu_s], 11264)[:, hc * DI + ic * 128: hc * DI + (ic + 1) * 128],
                                   Y[:, hc, tt * 512:(tt + 1) * 512],
                                   hc == 0, hc == ND - 1,
                                   waits=([("w%d" % wu_s, slot_ld[wu_s]), ufree[par]]
                                          if hc == 0 else ()),
                                   inc="pe" if hc == ND - 1 else None)
                        wu_last = u
                        sl = act(lambda h, par=par: h.activation(
                            out=SIL[:], in_=gb[par][:, 0:512],
                            func=AF.Silu), waits=[("pe", g), silfree[0]])
                        gfree[par] = ("act", sl)
                        hv = dve(lambda h, ic=ic, par=par: h.tensor_tensor(
                            out=big(BH, NI * 512)[:, ic * 512:(ic + 1) * 512],
                            in0=SIL[:], in1=ub[par][:, 0:512],
                            op=OP.mult),
                            waits=[("act", sl), ("pe", u), hfree])
                        ufree[par] = ("dve", hv)
                        silfree[0] = ("dve", hv)
                        h_dve.append(hv)
                    for tcl in range(4):
                        tc_g = tt * 4 + tcl
                        par = tc_g % 2
                        bank = PA if par == 0 else PBp
                        g2 = None
                        for half in range(2):
                            for ic in range(NI):
                                g2 = mm(bank[:, half * 512:(half + 1) * 512],
                                        big(BH, NI * 512)[:, ic * 512 + tcl * 128: ic * 512 + (tcl + 1) * 128],
                                        big(FSLOT[wd_s], 11264)[:, ic * D + half * 512: ic * D + (half + 1) * 512],
                                        ic == 0, ic == NI - 1,
                                        waits=[("dve", h_dve[ic])]
                                              + ([("w%d" % wd_s, slot_ld[wd_s]),
                                                  dfree[par]] if ic == 0 else []),
                                        inc="pe" if ic == NI - 1 else None)
                        wd_last = g2
                        scal = (CB[:, tc_g, e:e + 1] if e < 8
                                else CBSG[:, tc_g:tc_g + 1])
                        rv = dve(lambda h, tc_g=tc_g, b=bank, s=scal: h.scalar_tensor_tensor(
                            out=XRES[:, tc_g, :], in0=b[:], scalar=s,
                            in1=XRES[:, tc_g, :], op0=OP.mult, op1=OP.add),
                            waits=[("pe", g2), ("dve", router_dve),
                                   ("act", router_act)])
                        dfree[par] = ("dve", rv)
                        if e == NEXP - 1:
                            resid[tc_g] = ("dve", rv)
                    hfree = ("pe", p.cnt["pe"])
                slot_free[wg_s] = ("pe", wg_last)
                slot_free[wu_s] = ("pe", wu_last)
                slot_free[wd_s] = ("pe", wd_last)
                if e + 1 < NEXP:
                    for j in range(3):
                        s_i = kexp(e + 1, j)
                        rear = "(hc p) i -> p hc i" if j < 2 else "(ic p) o -> p ic o"
                        slot_ld[s_i] = wload(
                            big(FSLOT[s_i], 11264).rearrange("p (a b) -> p a b", a=(ND if j < 2 else NI)),
                            srcs(e + 1)[j].rearrange(rear, p=128), "w%d" % s_i,
                            waits=[slot_free[s_i]])
            free["PC"], free["PE2"] = gfree
            free["PD"], free["PF"] = ufree
            free["PA0"] = free["PA1"] = dfree[0]
            free["PB0"] = free["PB1"] = dfree[1]
            free["SIL0"] = silfree[0]
            return resid, p.cnt["pe"]

        xres_rdy = [("x", 0)] * NT
        last_resid = 0
        prev_store = None
        for s_i in range(n_seq):
            for j in range(4):
                lv = xdma(XRES[:, 4 * j:4 * j + 4, :],
                          dx[s_i].rearrange("(c p) d -> p c d", p=128)[:, 4 * j:4 * j + 4, :],
                          waits=([prev_store[j]] if prev_store else []),
                          key="x%d" % j)
                for tc in range(4 * j, 4 * j + 4):
                    xres_rdy[tc] = ("x%d" % j, lv)

            tmarks = lambda mk: ([("t%d" % i, mk["t%d" % i]) for i in range(4)]
                                 + [("t2", mk["t2"]), ("dve", mk["dve"]),
                                    ("act", mk["act"])])

            m = p.marks()
            free["BV_ones"] = ("pe", m["pe"])
            y_tp, tp_tc, ln_end = emit_ln(xres_rdy, m["pe"], tmarks(m))
            ms = emit_proj(wka, wva, wqa, y_tp, CONST[:, 2:3])
            xres_rdy = emit_attn("a", ms, woa, ln_end, m["dve"])

            if only_attn_a:
                last_resid = max(v for _, v in xres_rdy)
                for j in range(4):
                    xdma(dout[s_i].rearrange("(c p) d -> p c d", p=128)[:, 4 * j:4 * j + 4, :],
                         XRES[:, 4 * j:4 * j + 4, :],
                         waits=[xres_rdy[4 * j], xres_rdy[4 * j + 1],
                                xres_rdy[4 * j + 2], xres_rdy[4 * j + 3]])
                prev_store = p.cnt["x"]
                continue
            m = p.marks()
            free["SIL0"] = ("pe", m["pe"])
            y_tp, tp_tc, ln_end = emit_ln(xres_rdy, m["pe"], tmarks(m))
            xres_rdy, ffn_pe = emit_ffn(y_tp, tp_tc, m["pe"])

            m = p.marks()
            free["WA"] = ("pe", m["pe"])
            free["WB"] = ("pe", m["pe"])
            free["BV_ones"] = ("pe", m["pe"])
            y_tp, tp_tc, ln_end = emit_ln(xres_rdy, m["pe"], tmarks(m))
            ms = emit_proj(wkb, wvb, wqb, y_tp, CONST[:, 3:4])
            xres_rdy = emit_attn("b", ms, None, ln_end, m["dve"])

            m = p.marks()
            free["SIL0"] = ("pe", m["pe"])
            y_tp, tp_tc, ln_end = emit_ln(xres_rdy, m["pe"], tmarks(m))
            xres_rdy, ffn_pe = emit_ffn(y_tp, tp_tc, m["pe"])
            m = p.marks()
            free["WA"] = ("pe", m["pe"])
            free["WB"] = ("pe", m["pe"])
            last_resid = max(v for _, v in xres_rdy)

            store_of = []
            for j in range(4):
                sv = xdma(dout[s_i].rearrange("(c p) d -> p c d", p=128)[:, 4 * j:4 * j + 4, :],
                          XRES[:, 4 * j:4 * j + 4, :],
                          waits=[xres_rdy[4 * j], xres_rdy[4 * j + 1],
                                 xres_rdy[4 * j + 2], xres_rdy[4 * j + 3]],
                          key="x%d" % j)
                store_of.append(("x%d" % j, sv))
            prev_store = store_of

        for j in range(4):
            p.wait("sync", "x%d" % j, p.cnt["x%d" % j])
            p.wait("scalar", "x%d" % j, p.cnt["x%d" % j])

        with nc.Block() as block:
            def make_section(eng):
                def section(h):
                    for op in p.ops[eng]:
                        if op[0] == "w":
                            h.wait_ge(sems[op[1]], op[2])
                        else:
                            _, fn, inck, by = op
                            ins = fn(h)
                            if inck is not None:
                                ins.then_inc(sems[inck], by)
                return section
            for eng in ENGS:
                getattr(block, eng)(make_section(eng))

    return nc


_NC_CACHE = {}


def kernel(L0_attn_a_x, L0_attn_a_Wq, L0_attn_a_Wk, L0_attn_a_Wv, L0_attn_a_Wo,
           L0_attn_a_bo, L0_attn_a_denom_bias, L0_attn_b_Wq, L0_attn_b_Wk,
           L0_attn_b_Wv, L0_ffn_gate_w, L0_ffn_w_gate, L0_ffn_w_up,
           L0_ffn_w_down, L0_ffn_sw_gate, L0_ffn_sw_up, L0_ffn_sw_down,
           L0_ffn_shared_gate_w):
    x = np.asarray(L0_attn_a_x)
    tr = lambda w: np.ascontiguousarray(np.asarray(w).T)
    gw16 = np.zeros((16, D), np.float32)
    gw16[0:8] = np.asarray(L0_ffn_gate_w)
    gw16[8] = np.asarray(L0_ffn_shared_gate_w)[0]
    mask = np.triu(np.full((128, 128), -1e30, np.float32), k=1)

    common = {
        "wqa": tr(L0_attn_a_Wq), "wka": tr(L0_attn_a_Wk),
        "wva": tr(L0_attn_a_Wv), "woa": tr(L0_attn_a_Wo),
        "wqb": tr(L0_attn_b_Wq), "wkb": tr(L0_attn_b_Wk),
        "wvb": tr(L0_attn_b_Wv),
        "bo": np.ascontiguousarray(np.asarray(L0_attn_a_bo).reshape(1, D)),
        "db": np.ascontiguousarray(
            np.asarray(L0_attn_a_denom_bias).reshape(1, HEADS).astype(np.float32)),
        "gwt": tr(gw16),
        "w_gate": np.ascontiguousarray(L0_ffn_w_gate),
        "w_up": np.ascontiguousarray(L0_ffn_w_up),
        "w_down": np.ascontiguousarray(L0_ffn_w_down),
        "sw_gate": np.ascontiguousarray(L0_ffn_sw_gate),
        "sw_up": np.ascontiguousarray(L0_ffn_sw_up),
        "sw_down": np.ascontiguousarray(L0_ffn_sw_down),
        "mask": mask,
        "maskt": np.ascontiguousarray(mask.T),
        "idn": np.eye(128, dtype=ml_dtypes.bfloat16),
    }
    in_maps = []
    for c in range(NCORES):
        m = dict(common)
        m["x"] = np.ascontiguousarray(x[c * NSEQ:(c + 1) * NSEQ])
        in_maps.append(m)

    if "nc" not in _NC_CACHE:
        _NC_CACHE["nc"] = build_nc()
    nc = _NC_CACHE["nc"]
    res = run_bass_kernel_spmd(nc, in_maps, core_ids=list(range(NCORES)))
    return np.concatenate([np.asarray(r["out"], np.float32) for r in res.results],
                          axis=0)
